# version 13
# baseline (speedup 1.0000x reference)
import numpy as np
from contextlib import ExitStack

import concourse.bass as bass
import concourse.mybir as mybir
from concourse.bass_utils import run_bass_kernel_spmd

F32 = mybir.dt.float32
BF16 = mybir.dt.bfloat16
AF = mybir.ActivationFunctionType
ALU = mybir.AluOpType
AX = mybir.AxisListType

D = 2048
S = 2048
H = 16
FF = 5632
NFT = 44
NE = 8
NDT = 16
TOWN = 1024
EPS = 1e-5
ISQ = 0.08838834764831845
NEG = -1.0e30
ENGS = ("sync", "gpsimd", "tensor", "scalar", "vector")


class Ctr:
    __slots__ = ("sem", "n")

    def __init__(self, sem):
        self.sem = sem
        self.n = 0


class Bld:
    def __init__(self, nc):
        self.nc = nc
        self.ops = {k: [] for k in ENGS}
        self.lastwait = {k: {} for k in ENGS}

    def emit(self, eng, fn):
        self.ops[eng].append(fn)

    def wait(self, eng, ctr, val):
        assert 0 <= val <= ctr.n, f"{eng} wait {val} > emitted {ctr.n}"
        if val <= 0:
            return
        lw = self.lastwait[eng]
        if lw.get(id(ctr), -1) >= val:
            return
        lw[id(ctr)] = val
        self.emit(eng, lambda e, s=ctr.sem, v=val: e.wait_ge(s, v))

    def dma(self, eng, ctr, dst, src):
        self.emit(eng, lambda e, d=dst, s=src, sem=ctr.sem:
                  e.dma_start(d, s).then_inc(sem, 16))
        ctr.n += 16
        return ctr.n

    def mm(self, out, lhsT, rhs, start, stop, ctr=None, sgc=False):
        if ctr is None:
            self.emit("tensor", lambda e, o=out, l=lhsT, r=rhs, a=start, b_=stop, g=sgc:
                      e.matmul(o, l, r, start=a, stop=b_, skip_group_check=g))
            return None
        self.emit("tensor", lambda e, o=out, l=lhsT, r=rhs, a=start, b_=stop, g=sgc, s=ctr.sem:
                  e.matmul(o, l, r, start=a, stop=b_, skip_group_check=g).then_inc(s, 1))
        ctr.n += 1
        return ctr.n

    def act(self, out, in_, func, ctr=None, **kw):
        if ctr is None:
            self.emit("scalar", lambda e, o=out, i=in_, f=func, k=kw:
                      e.activation(o, i, f, **k))
            return None
        self.emit("scalar", lambda e, o=out, i=in_, f=func, k=kw, s=ctr.sem:
                  e.activation(o, i, f, **k).then_inc(s, 1))
        ctr.n += 1
        return ctr.n

    def scopy(self, out, in_, scale=None, ctr=None):
        if scale is None:
            fn = lambda e, o=out, i=in_: e.copy(o, i)
        else:
            fn = lambda e, o=out, i=in_, m=scale: e.mul(o, i, m)
        if ctr is None:
            self.emit("scalar", fn)
            return None
        self.emit("scalar", lambda e, f=fn, s=ctr.sem: f(e).then_inc(s, 1))
        ctr.n += 1
        return ctr.n

    def tt(self, out, in0, in1, op, ctr=None):
        fn = lambda e, o=out, a=in0, b_=in1, p=op: e.tensor_tensor(o, a, b_, p)
        if ctr is None:
            self.emit("vector", fn)
            return None
        self.emit("vector", lambda e, f=fn, s=ctr.sem: f(e).then_inc(s, 1))
        ctr.n += 1
        return ctr.n

    def vop(self, fn, ctr=None):
        if ctr is None:
            self.emit("vector", fn)
            return None
        self.emit("vector", lambda e, f=fn, s=ctr.sem: f(e).then_inc(s, 1))
        ctr.n += 1
        return ctr.n


def build_program():
    nc = bass.Bass("TRN2", target_bir_lowering=False)

    xT = nc.dram_tensor("xT", [D, S], F32, kind="ExternalInput")
    dmaskin = nc.dram_tensor("dmask", [2, 128, 256], F32, kind="ExternalInput")
    wqTb = nc.dram_tensor("wqTb", [4, 128, NDT, 512], F32, kind="ExternalInput")
    wkTb = nc.dram_tensor("wkTb", [4, 128, NDT, 512], F32, kind="ExternalInput")
    wvTb = nc.dram_tensor("wvTb", [4, 128, NDT, 512], F32, kind="ExternalInput")
    woTb = nc.dram_tensor("woTb", [NDT, 128, NDT, 128], F32, kind="ExternalInput")
    bo1p = nc.dram_tensor("bo1p", [1, D], F32, kind="ExternalInput")
    gateT = nc.dram_tensor("gateT", [128, NDT, NE], F32, kind="ExternalInput")
    w1Tb = nc.dram_tensor("w1Tb", [NFT, 128, NDT, 128], F32, kind="ExternalInput")
    w3Tb = nc.dram_tensor("w3Tb", [NFT, 128, NDT, 128], F32, kind="ExternalInput")
    w2Tb = nc.dram_tensor("w2Tb", [NDT, 128, NFT, 128], F32, kind="ExternalInput")
    a1sT = nc.dram_tensor("a1sT", [128, NDT, 128], F32, kind="ExternalInput")
    a3sT = nc.dram_tensor("a3sT", [128, NDT, 128], F32, kind="ExternalInput")
    b1sT = nc.dram_tensor("b1sT", [128, FF], F32, kind="ExternalInput")
    b3sT = nc.dram_tensor("b3sT", [128, FF], F32, kind="ExternalInput")
    a2b = nc.dram_tensor("a2b", [2, NE, 128, 22, 16], F32, kind="ExternalInput")
    b2stk = nc.dram_tensor("b2stk", [3, 128, D], F32, kind="ExternalInput")
    identin = nc.dram_tensor("identin", [128, 128], F32, kind="ExternalInput")
    onescolin = nc.dram_tensor("onescolin", [128, 1], F32, kind="ExternalInput")
    onesrowin = nc.dram_tensor("onesrowin", [1, 512], F32, kind="ExternalInput")
    trimaskin = nc.dram_tensor("trimaskin", [2, 128, 256], F32, kind="ExternalInput")
    selin = nc.dram_tensor("selin", [128, NE * 128], F32, kind="ExternalInput")
    out_h = nc.dram_tensor("out", [D, TOWN], F32, kind="ExternalOutput")
    x2s = nc.dram_tensor("x2s", [D, TOWN], F32)
    ln2s = nc.dram_tensor("ln2s", [D, TOWN], BF16)

    gctx = ExitStack()
    with gctx:
        E = gctx.enter_context
        b = Bld(nc)
        sd = Ctr(E(nc.semaphore("sd")))
        gd = Ctr(E(nc.semaphore("gd")))
        od = Ctr(E(nc.semaphore("od")))
        pe = Ctr(E(nc.semaphore("pe")))
        ve = Ctr(E(nc.semaphore("ve")))
        se = Ctr(E(nc.semaphore("se")))
        PB = [E(nc.psum_tensor(f"pb{i}", [128, 512], F32)) for i in range(8)]

        I_bf = E(nc.sbuf_tensor("I_bf", [128, 128], BF16))
        I_f = E(nc.sbuf_tensor("I_f", [128, 128], F32))
        onescol_bf = E(nc.sbuf_tensor("onescol_bf", [128, 1], BF16))
        onesrow = E(nc.sbuf_tensor("onesrow", [1, 512], F32))
        trimask_sb = E(nc.sbuf_tensor("trimask_sb", [128, 2 * 256], F32))
        dmask_sb = E(nc.sbuf_tensor("dmask_sb", [128, 2 * 256], F32))
        sel_sb = E(nc.sbuf_tensor("sel_sb", [128, NE * 128], BF16))

        b.dma("gpsimd", gd, I_bf[:, :], identin[:, :])
        b.dma("gpsimd", gd, onescol_bf[:, :], onescolin[:, :])
        b.dma("sync", sd, I_f[:, :], identin[:, :])
        b.dma("sync", sd, onesrow[:, :], onesrowin[:, :])
        b.dma("sync", sd, trimask_sb.rearrange("p (j m) -> p j m", j=2), trimaskin.rearrange("j p m -> p j m"))
        b.dma("sync", sd, dmask_sb.rearrange("p (j m) -> p j m", j=2), dmaskin.rearrange("j p m -> p j m"))
        b.dma("gpsimd", gd, sel_sb[:, :], selin[:, :])
        consts_gd = gd.n
        consts_sd = sd.n
        b.wait("tensor", gd, consts_gd)
        b.wait("tensor", sd, consts_sd)
        b.wait("vector", sd, consts_sd)
        b.wait("scalar", sd, consts_sd)

        arena = E(nc.sbuf_tensor("arena", [128, H * TOWN], BF16))
        attn_out = arena

        st_A = ExitStack()
        ln1 = st_A.enter_context(nc.sbuf_tensor("ln1", [128, NDT * S], BF16))
        st_B = ExitStack()
        E1 = st_B.enter_context
        xbf = [E1(nc.sbuf_tensor(f"xbf{i}", [128, S], BF16)) for i in range(2)]
        sqbf = [E1(nc.sbuf_tensor(f"sqbf{i}", [128, S], BF16)) for i in range(2)]
        rowm = E1(nc.sbuf_tensor("rowm", [1, S], F32))
        rowa = E1(nc.sbuf_tensor("rowa", [1, S], F32))
        rowb = E1(nc.sbuf_tensor("rowb", [1, S], F32))
        mb1 = E1(nc.sbuf_tensor("mb1", [128, S], F32))
        rb1 = E1(nc.sbuf_tensor("rb1", [128, S], F32))
        xf = [E1(nc.sbuf_tensor(f"xf{i}", [128, S], F32)) for i in range(2)]

        sq_done = [0, 0]
        mm_done = [0, 0]
        for dt in range(NDT):
            bi = dt % 2
            if dt >= 2:
                b.wait("gpsimd", ve, sq_done[bi])
                b.wait("gpsimd", pe, mm_done[bi])
            b.dma("gpsimd", gd, xbf[bi][:, :], xT[dt * 128:(dt + 1) * 128, :])
            b.wait("vector", gd, gd.n)
            if dt >= 2:
                b.wait("vector", pe, mm_done[bi])
            sq_done[bi] = b.tt(sqbf[bi][:, :], xbf[bi][:, :], xbf[bi][:, :], ALU.mult, ctr=ve)
            b.wait("tensor", ve, sq_done[bi])
            for c in range(4):
                b.mm(PB[c][0:1, :], onescol_bf[:, :], xbf[bi][:, c * 512:(c + 1) * 512],
                     start=(dt == 0), stop=(dt == NDT - 1))
                b.mm(PB[4 + c][0:1, :], onescol_bf[:, :], sqbf[bi][:, c * 512:(c + 1) * 512],
                     start=(dt == 0), stop=(dt == NDT - 1),
                     ctr=pe if c == 3 else None)
            mm_done[bi] = pe.n
        b.wait("scalar", pe, pe.n)
        for c in range(4):
            b.scopy(rowm[0:1, c * 512:(c + 1) * 512], PB[c][0:1, :], scale=1.0 / D)
            b.scopy(rowa[0:1, c * 512:(c + 1) * 512], PB[4 + c][0:1, :],
                    scale=1.0 / D, ctr=se)
        b.wait("vector", se, se.n)
        c = b.tt(rowb[0:1, :], rowm[0:1, :], rowm[0:1, :], ALU.mult, ctr=ve)
        b.wait("vector", ve, c)
        c = b.tt(rowa[0:1, :], rowa[0:1, :], rowb[0:1, :], ALU.subtract, ctr=ve)
        b.wait("vector", ve, c)
        b.vop(lambda e, o=rowa[0:1, :]: e.tensor_scalar(o, o, EPS, None, op0=ALU.add),
              ctr=ve)
        b.wait("scalar", ve, ve.n)
        b.act(rowb[0:1, :], rowa[0:1, :], AF.Ln)
        b.act(rowa[0:1, :], rowb[0:1, :], AF.Exp, scale=-0.5, ctr=se)
        b.wait("tensor", se, se.n)
        for c in range(4):
            b.mm(PB[c][:, :], onesrow[0:1, 0:128], rowm[0:1, c * 512:(c + 1) * 512],
                 start=True, stop=True)
            b.mm(PB[4 + c][:, :], onesrow[0:1, 0:128], rowa[0:1, c * 512:(c + 1) * 512],
                 start=True, stop=True, ctr=pe)
        b.wait("scalar", pe, pe.n)
        for c in range(4):
            b.scopy(mb1[:, c * 512:(c + 1) * 512], PB[c][:, :])
            b.scopy(rb1[:, c * 512:(c + 1) * 512], PB[4 + c][:, :], ctr=se)
        bc1_done = se.n
        ap_done = [0, 0]
        for dt in range(NDT):
            bi = dt % 2
            if dt >= 2:
                b.wait("sync", ve, ap_done[bi])
            b.dma("sync", sd, xf[bi][:, :], xT[dt * 128:(dt + 1) * 128, :])
            b.wait("vector", sd, sd.n)
            b.wait("vector", se, bc1_done)
            b.tt(xf[bi][:, :], xf[bi][:, :], mb1[:, :], ALU.subtract)
            ap_done[bi] = b.tt(ln1[:, dt * S:(dt + 1) * S], xf[bi][:, :], rb1[:, :],
                               ALU.mult, ctr=ve)
        ln1_done = ve.n
        st_B.close()

        qt_units = []
        for i in range(4):
            us = []
            for kt in range(2 * i + 2):
                mk = None
                if kt == 2 * i:
                    mk = ("t", 0)
                elif kt == 2 * i + 1:
                    mk = ("t", 1)
                us.append((kt, mk))
            for j in range(2 * i + 2):
                mk = None
                if j == 2 * i:
                    mk = ("d", 0)
                elif j == 2 * i + 1:
                    mk = ("d", 1)
                us.append((8 + j, mk))
            qt_units.append(us)

        st_C = ExitStack()
        E2 = st_C.enter_context
        wbuf = [E2(nc.sbuf_tensor(f"wbuf{i}", [128, NDT * 512], BF16)) for i in range(2)]
        k_bf = E2(nc.sbuf_tensor("k_bf", [128, 4 * S], BF16))
        v_sb = E2(nc.sbuf_tensor("v_sb", [128, 16 * 512], BF16))
        q_bf = E2(nc.sbuf_tensor("q_bf", [128, 4 * TOWN], BF16))
        ebuf = [E2(nc.sbuf_tensor(f"ebuf{i}", [128, 512], BF16)) for i in range(2)]
        r_sb = E2(nc.sbuf_tensor("r_sb", [1, 256], F32))
        bc_sb = [E2(nc.sbuf_tensor(f"bc_sb{i}", [128, 256], F32)) for i in range(2)]

        wbuf_free = [0, 0]
        bank_ev = [0, 0]
        prev_att = [0, 0]
        prev_rec = [0, 0]
        prev_bcc = [0, 0]
        sc_cons = [0, 0]
        prev_avd_last = 0
        i_hq = 0
        for g in range(4):
            for wi, (wsrc, kind) in enumerate(((wkTb, "k"), (wvTb, "v"), (wqTb, "q"))):
                bi = (g * 3 + wi) % 2
                b.wait("gpsimd", pe, wbuf_free[bi])
                b.wait("gpsimd", ve, ln1_done)
                b.dma("gpsimd", gd,
                      wbuf[bi].rearrange("p (dt m) -> p dt m", dt=NDT),
                      wsrc[g])
                loaded = gd.n
                b.wait("tensor", gd, loaded)
                b.wait("tensor", ve, ln1_done)
                if kind == "k":
                    for mt in range(4):
                        for tc in range(4):
                            bk = (mt * 4 + tc) % 2
                            b.wait("tensor", se, bank_ev[bk])
                            for dt in range(NDT):
                                b.mm(PB[bk][:, :],
                                     wbuf[bi][:, dt * 512 + mt * 128:dt * 512 + (mt + 1) * 128],
                                     ln1[:, dt * S + tc * 512:dt * S + (tc + 1) * 512],
                                     start=(dt == 0), stop=(dt == NDT - 1),
                                     ctr=pe if dt == NDT - 1 else None)
                            b.wait("scalar", pe, pe.n)
                            bank_ev[bk] = b.scopy(
                                k_bf[:, mt * S + tc * 512:mt * S + (tc + 1) * 512],
                                PB[bk][:, :], ctr=se)
                elif kind == "v":
                    for tt in range(16):
                        bk = tt % 2
                        b.wait("tensor", se, bank_ev[bk])
                        for dt in range(NDT):
                            b.mm(PB[bk][:, :],
                                 ln1[:, dt * S + tt * 128:dt * S + (tt + 1) * 128],
                                 wbuf[bi][:, dt * 512:(dt + 1) * 512],
                                 start=(dt == 0), stop=(dt == NDT - 1),
                                 ctr=pe if dt == NDT - 1 else None)
                        b.wait("scalar", pe, pe.n)
                        bank_ev[bk] = b.scopy(v_sb[:, tt * 512:(tt + 1) * 512],
                                              PB[bk][:, :], ctr=se)
                else:
                    for mt in range(4):
                        for tc in range(2):
                            bk = (mt * 2 + tc) % 2
                            b.wait("tensor", se, bank_ev[bk])
                            for dt in range(NDT):
                                b.mm(PB[bk][:, :],
                                     wbuf[bi][:, dt * 512 + mt * 128:dt * 512 + (mt + 1) * 128],
                                     ln1[:, dt * S + tc * 512:dt * S + (tc + 1) * 512],
                                     start=(dt == 0), stop=(dt == NDT - 1),
                                     ctr=pe if dt == NDT - 1 else None)
                            b.wait("scalar", pe, pe.n)
                            bank_ev[bk] = b.scopy(
                                q_bf[:, mt * TOWN + tc * 512:mt * TOWN + (tc + 1) * 512],
                                PB[bk][:, :], ctr=se)
                wbuf_free[bi] = pe.n
            qkv_ev = se.n
            sc_cons = [max(sc_cons[0], bank_ev[0]), max(sc_cons[1], bank_ev[1])]
            for hl in range(4):
                hglob = g * 4 + hl
                for pi in range(4):
                    units = qt_units[pi]
                    U = len(units)
                    ai = i_hq % 2
                    avb = PB[2 + ai]
                    dnb = PB[6 + ai]
                    bcb = PB[4 + ai]
                    NP_ = U // 2
                    expp_cnt = [0] * NP_
                    scr_cnt = [0] * U
                    avd_cnt = [0] * U
                    def emit_avd(w):
                        ktw, _ = units[w]
                        if w == 0:
                            b.wait("tensor", ve, prev_att[ai])
                            b.wait("tensor", ve, prev_rec[ai])
                        b.wait("tensor", se, expp_cnt[w // 2])
                        b.mm(avb[:, 0:256],
                             v_sb[:, ktw * 512 + hl * 128:ktw * 512 + (hl + 1) * 128],
                             ebuf[(w // 2) % 2][:, (w % 2) * 256:(w % 2) * 256 + 256],
                             start=(w == 0), stop=(w == U - 1))
                        avd_cnt[w] = b.mm(dnb[0:1, 0:256], onescol_bf[:, :],
                                          ebuf[(w // 2) % 2][:, (w % 2) * 256:(w % 2) * 256 + 256],
                                          start=(w == 0), stop=(w == U - 1), ctr=pe)
                    mv = 0
                    for u in range(U):
                        kt, mk = units[u]
                        pr = u // 2
                        bank = PB[pr % 2]
                        coff = (u % 2) * 256
                        if u % 2 == 0:
                            b.wait("tensor", se, sc_cons[pr % 2])
                            b.wait("tensor", se, qkv_ev)
                        scr_cnt[u] = b.mm(
                            bank[:, coff:coff + 256],
                            k_bf[:, hl * S + kt * 128:hl * S + (kt + 1) * 128],
                            q_bf[:, hl * TOWN + pi * 256:hl * TOWN + (pi + 1) * 256],
                            start=True, stop=True, ctr=pe)
                        if mk is not None:
                            msk = trimask_sb if mk[0] == "t" else dmask_sb
                            moff = mk[1] * 256
                            b.wait("vector", pe, scr_cnt[u])
                            mv = b.tt(bank[:, coff:coff + 256], bank[:, coff:coff + 256],
                                      msk[:, moff:moff + 256], ALU.add, ctr=ve)
                        if u % 2 == 1:
                            b.wait("scalar", pe, scr_cnt[u])
                            b.wait("scalar", ve, mv)
                            b.wait("scalar", pe,
                                   avd_cnt[2 * (pr - 2) + 1] if pr >= 2 else prev_avd_last)
                            expp_cnt[pr] = b.act(ebuf[pr % 2][:, :], bank[:, 0:512],
                                                 AF.Exp, scale=ISQ, ctr=se)
                            sc_cons[pr % 2] = expp_cnt[pr]
                            if pr >= 1:
                                emit_avd(u - 3)
                                emit_avd(u - 2)
                    emit_avd(U - 2)
                    emit_avd(U - 1)
                    prev_avd_last = avd_cnt[U - 1]
                    b.wait("vector", pe, avd_cnt[U - 1])
                    prev_rec[ai] = b.vop(
                        lambda e, o=r_sb[0:1, :], i=dnb[0:1, 0:256]: e.reciprocal(o, i),
                        ctr=ve)
                    b.wait("tensor", ve, prev_rec[ai])
                    b.wait("tensor", se, prev_bcc[ai])
                    bmn = b.mm(bcb[:, 0:256], onesrow[0:1, 0:128], r_sb[0:1, :],
                               start=True, stop=True, ctr=pe)
                    b.wait("scalar", pe, bmn)
                    prev_bcc[ai] = b.scopy(bc_sb[ai][:, :], bcb[:, 0:256], ctr=se)
                    b.wait("vector", se, prev_bcc[ai])
                    prev_att[ai] = b.tt(
                        attn_out[:, hglob * TOWN + pi * 256:hglob * TOWN + (pi + 1) * 256],
                        avb[:, 0:256], bc_sb[ai][:, :], ALU.mult, ctr=ve)
                    i_hq += 1
        att_done = ve.n
        att_pe = pe.n
        att_se = se.n
        st_C.close()
        st_A.close()

        st_H = ExitStack()
        EH = st_H.enter_context
        wfm = EH(nc.sbuf_tensor("wfm", [128, TOWN], BF16))
        wb_bf = EH(nc.sbuf_tensor("wb_bf", [128, NE * 512], BF16))
        u1_sb = EH(nc.sbuf_tensor("u1_sb", [128, TOWN], BF16))
        u3_sb = EH(nc.sbuf_tensor("u3_sb", [128, TOWN], BF16))
        st_E = ExitStack()
        x2_sb = st_E.enter_context(nc.sbuf_tensor("x2_sb", [128, NDT * TOWN], F32))
        st_D = ExitStack()
        E3 = st_D.enter_context
        wobuf = [E3(nc.sbuf_tensor(f"wobuf{i}", [128, NDT * 128], BF16)) for i in range(2)]
        x_own = E3(nc.sbuf_tensor("x_own", [128, NDT * TOWN], F32))
        bo_sb = E3(nc.sbuf_tensor("bo_sb", [1, D], F32))

        b.wait("sync", pe, att_pe)
        b.wait("sync", ve, att_done)
        b.wait("sync", se, att_se)
        b.dma("sync", sd, bo_sb[:, :], bo1p[:, :])
        for dt in range(NDT):
            b.dma("sync", sd, x_own[:, dt * TOWN:(dt + 1) * TOWN],
                  xT[dt * 128:(dt + 1) * 128, 0:TOWN])
        xown_ld = sd.n
        wo_free = [0, 0]
        wo_bank = [att_done, att_done]
        x2done = 0
        for dt in range(NDT):
            bi = dt % 2
            b.wait("gpsimd", pe, wo_free[bi])
            b.wait("gpsimd", ve, att_done)
            b.dma("gpsimd", gd,
                  wobuf[bi].rearrange("p (kt m) -> p kt m", kt=NDT),
                  woTb[dt])
            lded = gd.n
            b.wait("tensor", gd, lded)
            b.wait("tensor", ve, att_done)
            b.wait("tensor", se, att_se)
            for tc in range(2):
                bk = (dt * 2 + tc) % 2
                b.wait("tensor", ve, wo_bank[bk])
                for kt in range(NDT):
                    b.mm(PB[bk][:, :],
                         wobuf[bi][:, kt * 128:(kt + 1) * 128],
                         attn_out[:, kt * TOWN + tc * 512:kt * TOWN + (tc + 1) * 512],
                         start=(kt == 0), stop=False)
                pc = b.mm(PB[bk][:, :], bo_sb[0:1, dt * 128:(dt + 1) * 128],
                          onesrow[0:1, :], start=False, stop=True, ctr=pe)
                b.wait("vector", pe, pc)
                b.wait("vector", sd, xown_ld)
                x2done = b.tt(
                    x2_sb[:, dt * TOWN + tc * 512:dt * TOWN + (tc + 1) * 512],
                    PB[bk][:, :],
                    x_own[:, dt * TOWN + tc * 512:dt * TOWN + (tc + 1) * 512],
                    ALU.add, ctr=ve)
                wo_bank[bk] = x2done
            wo_free[bi] = pe.n
        x2_done = ve.n
        wo_pe = pe.n
        st_D.close()

        st_G = ExitStack()
        ln2 = st_G.enter_context(nc.sbuf_tensor("ln2", [128, NDT * TOWN], BF16))
        st_F = ExitStack()
        E4 = st_F.enter_context
        x2bf = E4(nc.sbuf_tensor("x2bf", [128, TOWN], BF16))
        sq2 = E4(nc.sbuf_tensor("sq2", [128, TOWN], BF16))
        rowm2 = E4(nc.sbuf_tensor("rowm2", [1, TOWN], F32))
        rowa2 = E4(nc.sbuf_tensor("rowa2", [1, TOWN], F32))
        rowb2 = E4(nc.sbuf_tensor("rowb2", [1, TOWN], F32))
        mb2 = E4(nc.sbuf_tensor("mb2", [128, TOWN], F32))
        rb2 = E4(nc.sbuf_tensor("rb2", [128, TOWN], F32))

        b.wait("sync", ve, x2_done)
        x2sp = b.dma("sync", sd, x2s.rearrange("(dt p) t -> p dt t", p=128),
                     x2_sb.rearrange("p (dt t) -> p dt t", dt=NDT))

        mm2_done = 0
        for dt in range(NDT):
            sl = x2_sb[:, dt * TOWN:(dt + 1) * TOWN]
            b.wait("vector", ve, x2_done)
            if dt >= 1:
                b.wait("vector", pe, mm2_done)
            b.vop(lambda e, o=x2bf[:, :], i=sl: e.tensor_copy(o, i))
            sqd = b.tt(sq2[:, :], sl, sl, ALU.mult, ctr=ve)
            b.wait("tensor", ve, sqd)
            b.wait("tensor", ve, x2_done)
            b.wait("tensor", se, att_se)
            for c in range(2):
                b.mm(PB[c][0:1, :], onescol_bf[:, :], x2bf[:, c * 512:(c + 1) * 512],
                     start=(dt == 0), stop=(dt == NDT - 1))
                b.mm(PB[2 + c][0:1, :], onescol_bf[:, :], sq2[:, c * 512:(c + 1) * 512],
                     start=(dt == 0), stop=(dt == NDT - 1),
                     ctr=pe if c == 1 else None)
            mm2_done = pe.n
        b.wait("scalar", pe, mm2_done)
        for c in range(2):
            b.scopy(rowm2[0:1, c * 512:(c + 1) * 512], PB[c][0:1, :], scale=1.0 / D)
            b.scopy(rowa2[0:1, c * 512:(c + 1) * 512], PB[2 + c][0:1, :],
                    scale=1.0 / D, ctr=se)
        b.wait("vector", se, se.n)
        c = b.tt(rowb2[0:1, :], rowm2[0:1, :], rowm2[0:1, :], ALU.mult, ctr=ve)
        b.wait("vector", ve, c)
        c = b.tt(rowa2[0:1, :], rowa2[0:1, :], rowb2[0:1, :], ALU.subtract, ctr=ve)
        b.wait("vector", ve, c)
        b.vop(lambda e, o=rowa2[0:1, :]: e.tensor_scalar(o, o, EPS, None, op0=ALU.add),
              ctr=ve)
        b.wait("scalar", ve, ve.n)
        b.act(rowb2[0:1, :], rowa2[0:1, :], AF.Ln)
        b.act(rowa2[0:1, :], rowb2[0:1, :], AF.Exp, scale=-0.5, ctr=se)
        b.wait("tensor", se, se.n)
        for c in range(2):
            b.mm(PB[c][:, :], onesrow[0:1, 0:128], rowm2[0:1, c * 512:(c + 1) * 512],
                 start=True, stop=True)
            b.mm(PB[2 + c][:, :], onesrow[0:1, 0:128], rowa2[0:1, c * 512:(c + 1) * 512],
                 start=True, stop=True, ctr=pe)
        b.wait("scalar", pe, pe.n)
        for c in range(2):
            b.scopy(mb2[:, c * 512:(c + 1) * 512], PB[c][:, :])
            b.scopy(rb2[:, c * 512:(c + 1) * 512], PB[2 + c][:, :], ctr=se)
        bc2_done = se.n
        ln2_done = 0
        b.wait("vector", se, bc2_done)
        b.wait("vector", sd, x2sp)
        for dt in range(NDT):
            sl = x2_sb[:, dt * TOWN:(dt + 1) * TOWN]
            b.tt(sl, sl, mb2[:, :], ALU.subtract)
            ln2_done = b.tt(ln2[:, dt * TOWN:(dt + 1) * TOWN], sl, rb2[:, :],
                            ALU.mult, ctr=ve)
        b.wait("sync", ve, ln2_done)
        ln2sp = b.dma("sync", sd, ln2s.rearrange("(dt p) t -> p dt t", p=128),
                      ln2.rearrange("p (dt t) -> p dt t", dt=NDT))
        st_F.close()

        st_I = ExitStack()
        E5 = st_I.enter_context
        gate_bf = E5(nc.sbuf_tensor("gate_bf", [128, NDT * NE], BF16))
        logit = E5(nc.sbuf_tensor("logit", [128, 8 * NE], F32))
        grows = E5(nc.sbuf_tensor("grows", [128, 8], F32))
        gtmp = E5(nc.sbuf_tensor("gtmp", [128, 4 * NE], F32))

        b.wait("gpsimd", ve, ln2_done)
        b.wait("gpsimd", sd, x2sp)
        b.dma("gpsimd", gd, gate_bf.rearrange("p (dt e) -> p dt e", dt=NDT),
              gateT[:, :, :])
        gld = gd.n
        zr = b.vop(lambda e, o=wfm[:, :]: e.memset(o, 0.0), ctr=ve)
        b.wait("tensor", gd, gld)
        b.wait("tensor", ve, ln2_done)
        b.wait("tensor", se, se.n)
        gevac = [0, 0]
        for tt in range(8):
            bk = 4 + tt % 2
            b.wait("tensor", se, gevac[tt % 2])
            for dt in range(NDT):
                b.mm(PB[bk][:, 0:8],
                     ln2[:, dt * TOWN + tt * 128:dt * TOWN + (tt + 1) * 128],
                     gate_bf[:, dt * NE:(dt + 1) * NE],
                     start=(dt == 0), stop=(dt == NDT - 1),
                     ctr=pe if dt == NDT - 1 else None)
            b.wait("scalar", pe, pe.n)
            gevac[tt % 2] = b.scopy(logit[:, tt * NE:(tt + 1) * NE], PB[bk][:, 0:8], ctr=se)
        b.wait("vector", se, se.n)
        wfm_evac = 0
        for tt in range(8):
            lg = logit[:, tt * NE:(tt + 1) * NE]
            msd = gtmp[:, 2 * NE:3 * NE]
            wtm = gtmp[:, 3 * NE:4 * NE]
            eq1 = gtmp[:, 0:NE]
            eq2 = gtmp[:, NE:2 * NE]
            m1 = grows[:, 0:1]
            m2 = grows[:, 1:2]
            dd = grows[:, 2:3]
            dn = grows[:, 3:4]
            r1 = grows[:, 4:5]
            r2 = grows[:, 5:6]
            negc = grows[:, 6:7]
            c = b.vop(lambda e, o=m1, i=lg: e.reduce_max(o, i, axis=AX.X), ctr=ve)
            b.wait("vector", ve, c)
            c = b.vop(lambda e, o=eq1, i=lg, s=m1:
                      e.tensor_tensor(o, i, s.broadcast_to([128, NE]), ALU.is_equal),
                      ctr=ve)
            b.wait("vector", ve, c)
            c = b.vop(lambda e, o=msd, i=eq1, s=lg:
                      e.scalar_tensor_tensor(o, i, NEG, s, op0=ALU.mult, op1=ALU.add),
                      ctr=ve)
            b.wait("vector", ve, c)
            c = b.vop(lambda e, o=m2, i=msd: e.reduce_max(o, i, axis=AX.X), ctr=ve)
            b.wait("vector", ve, c)
            c = b.vop(lambda e, o=eq2, i=msd, s=m2:
                      e.tensor_tensor(o, i, s.broadcast_to([128, NE]), ALU.is_equal),
                      ctr=ve)
            b.wait("vector", ve, c)
            c = b.tt(dd, m1, m2, ALU.subtract, ctr=ve)
            b.wait("vector", ve, c)
            dv = b.vop(lambda e, o=dn, i=dd:
                       e.tensor_scalar(o, i, -1.0, None, op0=ALU.mult), ctr=ve)
            b.wait("vector", ve, dv)
            b.wait("scalar", ve, dv)
            b.act(r1, dd, AF.Sigmoid)
            sg = b.act(r2, dn, AF.Sigmoid, ctr=se)
            b.wait("vector", se, sg)
            c = b.vop(lambda e, o=wtm, i=eq1, s=r1:
                      e.tensor_scalar(o, i, s, None, op0=ALU.mult), ctr=ve)
            b.wait("vector", ve, c)
            wv = b.vop(lambda e, o=wtm, i=eq2, s=r2, w=wtm:
                       e.scalar_tensor_tensor(o, i, s, w, op0=ALU.mult, op1=ALU.add),
                       ctr=ve)
            b.wait("vector", ve, wv)
            b.wait("tensor", ve, wv)
            b.wait("tensor", ve, zr)
            b.wait("tensor", se, wfm_evac)
            tm = b.mm(PB[6][0:8, 0:128], wtm, I_f[:, :], start=True, stop=True, ctr=pe)
            b.wait("scalar", pe, tm)
            wfm_evac = b.scopy(wfm[0:8, tt * 128:(tt + 1) * 128], PB[6][0:8, 0:128],
                               ctr=se)
        gate_done_se = wfm_evac
        st_I.close()

        st_J = ExitStack()
        E6a = st_J.enter_context
        a1buf = E6a(nc.sbuf_tensor("a1buf", [128, NDT * 128], BF16))
        a3buf = E6a(nc.sbuf_tensor("a3buf", [128, NDT * 128], BF16))
        b.wait("gpsimd", ve, ve.n)
        b.wait("gpsimd", pe, pe.n)
        b.dma("gpsimd", gd, a1buf.rearrange("p (dt m) -> p dt m", dt=NDT),
              a1sT[:, :, :])
        b.dma("gpsimd", gd, a3buf.rearrange("p (dt m) -> p dt m", dt=NDT),
              a3sT[:, :, :])
        b.wait("tensor", gd, gd.n)
        b.wait("tensor", se, gate_done_se)
        for (abuf, usb) in ((a1buf, u1_sb), (a3buf, u3_sb)):
            for tc in range(2):
                bk = 4 + tc
                for dt in range(NDT):
                    b.mm(PB[bk][:, :], abuf[:, dt * 128:(dt + 1) * 128],
                         ln2[:, dt * TOWN + tc * 512:dt * TOWN + (tc + 1) * 512],
                         start=(dt == 0), stop=(dt == NDT - 1),
                         ctr=pe if dt == NDT - 1 else None)
                b.wait("scalar", pe, pe.n)
                b.scopy(usb[:, tc * 512:(tc + 1) * 512], PB[bk][:, :], scale=2.0, ctr=se)
        u_done = se.n
        u_pe = pe.n
        st_J.close()
        st_G.close()
        st_E.close()

        st_K = ExitStack()
        EK = st_K.enter_context
        b1h = EK(nc.sbuf_tensor("b1h", [128, 22 * 128], BF16))
        b3h = EK(nc.sbuf_tensor("b3h", [128, 22 * 128], BF16))
        a2h = EK(nc.sbuf_tensor("a2h", [128, 22 * NE * 16], BF16))
        b2s = EK(nc.sbuf_tensor("b2s", [128, 3 * D], BF16))
        base1_bf = EK(nc.sbuf_tensor("base1_bf", [128, 512], BF16))
        base3_bf = EK(nc.sbuf_tensor("base3_bf", [128, 512], BF16))
        s1b = [EK(nc.sbuf_tensor(f"s1b{i}", [128, 512], BF16)) for i in range(2)]
        wactb = [EK(nc.sbuf_tensor(f"wactb{i}", [128, 512], BF16)) for i in range(2)]
        w1b = [EK(nc.sbuf_tensor(f"w1b{i}", [128, NDT * 128], BF16)) for i in range(2)]
        w3b = [EK(nc.sbuf_tensor(f"w3b{i}", [128, NDT * 128], BF16)) for i in range(2)]
        mix = EK(nc.sbuf_tensor("mix", [128, NFT * 512], BF16))
        u2w = EK(nc.sbuf_tensor("u2w", [128, 3 * 512], BF16))
        w2st = [EK(nc.sbuf_tensor(f"w2st{i}", [128, NFT * 128], BF16)) for i in range(2)]
        x2ch = [EK(nc.sbuf_tensor(f"x2ch{i}", [128, 512], F32)) for i in range(2)]

        b.wait("gpsimd", se, u_done)
        b.wait("gpsimd", pe, u_pe)
        b.wait("gpsimd", sd, ln2sp)
        b.wait("sync", pe, u_pe)
        b.wait("vector", pe, u_pe)
        b.dma("gpsimd", gd, b2s.rearrange("p (j m) -> p j m", j=3),
              b2stk.rearrange("j p m -> p j m"))
        b2s_ld = gd.n

        U2 = (PB[0], PB[1], PB[2])
        MIXB = PB[3]
        T0, T1, T2, T3 = PB[4], PB[5], PB[6], PB[7]

        w1_free = [0, 0]
        w2_free = [0, 0]
        lora_free = 0
        ln2th_free = 0
        uz_free = 0
        u2_memset_gate_se = u_done
        mix_start_se = u_done
        freeT0_se = u_done
        freeT1_ve = 0
        freeT2_se = u_done
        freeT3_ve = 0
        freeT2_ve = 0
        freeT3_ve2 = 0
        base1_gate_pe = 0
        base3_gate_pe = 0
        wb_gate_ve = 0
        u2w_gate_pe = 0
        mixbuf_gate_pe = 0
        prev_out_ve = [0, 0]
        prev_x2_ve = [0, 0]
        prev_store = [0, 0]
        mixevac_last = 0

        for th in range(2):
            b.wait("sync", pe, ln2th_free)
            b.wait("sync", sd, ln2sp)
            b.dma("sync", sd, arena[:, 0:8192].rearrange("p (dt t) -> p dt t", dt=NDT),
                  ln2s.rearrange("(dt p) t -> p dt t", p=128)[:, :, th * 512:(th + 1) * 512])
            ln2th_ld = sd.n

            b.wait("vector", se, u_done)
            b.wait("vector", pe, uz_free)
            b.vop(lambda e, o=arena[:, 8192:12288]: e.memset(o, 0.0))
            z3 = b.vop(lambda e, o=arena[:, 12288:16384]: e.memset(o, 0.0), ctr=ve)
            b.wait("gpsimd", ve, z3)
            for e_ in range(NE):
                b.dma("gpsimd", gd,
                      arena[e_ * 16:e_ * 16 + 16, 8192 + e_ * 512:8192 + (e_ + 1) * 512],
                      u1_sb[e_ * 16:e_ * 16 + 16, th * 512:(th + 1) * 512])
                b.dma("gpsimd", gd,
                      arena[e_ * 16:e_ * 16 + 16, 12288 + e_ * 512:12288 + (e_ + 1) * 512],
                      u3_sb[e_ * 16:e_ * 16 + 16, th * 512:(th + 1) * 512])
            uz_ld = gd.n

            b.wait("tensor", se, gate_done_se)
            b.wait("tensor", ve, wb_gate_ve)
            wbev = 0
            for e_ in range(NE):
                bk = T2 if e_ % 2 == 0 else T3
                b.wait("tensor", se, freeT2_se if e_ % 2 == 0 else u_done)
                b.wait("tensor", ve, freeT2_ve if e_ % 2 == 0 else freeT3_ve2)
                b.wait("tensor", se, wbev)
                bmw = b.mm(bk[:, :], sel_sb[:, e_ * 128:(e_ + 1) * 128],
                           wfm[:, th * 512:(th + 1) * 512], start=True, stop=True, ctr=pe)
                b.wait("scalar", pe, bmw)
                wbev = b.scopy(wb_bf[:, e_ * 512:(e_ + 1) * 512], bk[:, :], ctr=se)
            wb_done = wbev
            freeT2_se = wb_done

            b.wait("vector", se, u2_memset_gate_se)
            b.wait("vector", pe, u2w_gate_pe)
            for j in range(3):
                zz = b.vop(lambda e, o=U2[j][:, :]: e.memset(o, 0.0),
                           ctr=ve if j == 2 else None)
            u2z_done = ve.n

            prev_silu = [wb_done] * NE
            prev_act = [0] * NE
            prev_wact7 = 0
            first_unit = True
            for sw in range(2):
                b.wait("gpsimd", pe, lora_free)
                fh = sw * 22
                b.dma("gpsimd", gd, b1h[:, :], b1sT[:, fh * 128:(fh + 22) * 128])
                b.dma("gpsimd", gd, b3h[:, :], b3sT[:, fh * 128:(fh + 22) * 128])
                a2h4 = a2h.rearrange("p (e f r) -> p e f r", e=NE, f=22)
                for e_ in range(NE):
                    b.dma("gpsimd", gd, a2h4[:, e_, :, :], a2b[sw, e_])
                lora_ld = gd.n

                for ft in range(22):
                    Fg = fh + ft
                    bi = ft % 2
                    b.wait("gpsimd", pe, w1_free[bi])
                    b.dma("gpsimd", gd, w1b[bi].rearrange("p (dt m) -> p dt m", dt=NDT),
                          w1Tb[Fg])
                    b.dma("gpsimd", gd, w3b[bi].rearrange("p (dt m) -> p dt m", dt=NDT),
                          w3Tb[Fg])
                    wld = gd.n

                    b.wait("tensor", gd, wld)
                    b.wait("tensor", sd, ln2th_ld)
                    b.wait("tensor", se, freeT0_se)
                    b.wait("tensor", se, u_done)
                    for dt in range(NDT):
                        b.mm(T0[:, :], w1b[bi][:, dt * 128:(dt + 1) * 128],
                             arena[:, dt * 512:(dt + 1) * 512],
                             start=(dt == 0), stop=(dt == NDT - 1),
                             ctr=pe if dt == NDT - 1 else None)
                    p_b1 = pe.n
                    b.wait("scalar", pe, p_b1)
                    b.wait("scalar", pe, base1_gate_pe)
                    b1_ev = b.scopy(base1_bf[:, :], T0[:, :], ctr=se)
                    b.wait("tensor", ve, freeT1_ve)
                    for dt in range(NDT):
                        b.mm(T1[:, :], w3b[bi][:, dt * 128:(dt + 1) * 128],
                             arena[:, dt * 512:(dt + 1) * 512],
                             start=(dt == 0), stop=(dt == NDT - 1),
                             ctr=pe if dt == NDT - 1 else None)
                    p_b3 = pe.n
                    b.wait("scalar", pe, p_b3)
                    b.wait("scalar", pe, base3_gate_pe)
                    b3_ev = b.scopy(base3_bf[:, :], T1[:, :], ctr=se)
                    w1_free[bi] = p_b3

                    b.wait("tensor", gd, lora_ld)
                    b.wait("tensor", gd, uz_ld)
                    b.wait("tensor", gd, b2s_ld)
                    silu_cnt = [0] * NE
                    act_cnt = [0] * NE
                    wact_cnt = [0] * NE
                    h3_cnt = [0] * NE
                    mixu_cnt = [0] * NE

                    def emit_mixu(w, _sw=sw, _ft=ft):
                        b.wait("tensor", ve, wact_cnt[w])
                        if w == 0:
                            b.wait("tensor", se, mix_start_se)
                            b.wait("tensor", ve, u2z_done)
                        b.mm(MIXB[:, :], I_bf[:, :], wactb[w % 2][:, :],
                             start=(w == 0), stop=(w == NE - 1))
                        j, kk = w // 3, w % 3
                        mixu_cnt[w] = b.mm(
                            U2[j][32 * kk:32 * kk + 16, :],
                            a2h[:, (w * 22 + _ft) * 16:(w * 22 + _ft + 1) * 16],
                            wactb[w % 2][:, :],
                            start=False, stop=(_sw == 1 and _ft == 21),
                            ctr=pe, sgc=True)

                    for e_ in range(NE):
                        h1b = T2 if e_ % 2 == 0 else T0
                        h3b = T3 if e_ % 2 == 0 else T1
                        b.wait("tensor", se, b1_ev)
                        b.wait("tensor", se, b3_ev)
                        if e_ >= 1:
                            b.wait("tensor", ve, wact_cnt[e_ - 1])
                        else:
                            b.wait("tensor", ve, prev_wact7)
                            b.wait("tensor", se, prev_silu[7])
                            if first_unit:
                                b.wait("tensor", ve, freeT2_ve)
                                b.wait("tensor", ve, freeT3_ve2)
                        b.mm(h1b[:, :], I_bf[:, :], base1_bf[:, :], start=True, stop=False)
                        h1c = b.mm(h1b[:, :], b1h[:, ft * 128:(ft + 1) * 128],
                                   arena[:, 8192 + e_ * 512:8192 + (e_ + 1) * 512],
                                   start=False, stop=True, ctr=pe)
                        b.mm(h3b[:, :], I_bf[:, :], base3_bf[:, :], start=True, stop=False)
                        h3_cnt[e_] = b.mm(h3b[:, :], b3h[:, ft * 128:(ft + 1) * 128],
                                          arena[:, 12288 + e_ * 512:12288 + (e_ + 1) * 512],
                                          start=False, stop=True, ctr=pe)
                        b.wait("scalar", pe, h1c)
                        if e_ >= 2:
                            b.wait("scalar", ve, act_cnt[e_ - 2])
                        silu_cnt[e_] = b.act(s1b[e_ % 2][:, :], h1b[:, :], AF.Silu, ctr=se)
                        b.wait("vector", se, silu_cnt[e_])
                        b.wait("vector", se, wb_done)
                        b.tt(s1b[e_ % 2][:, :], s1b[e_ % 2][:, :],
                             wb_bf[:, e_ * 512:(e_ + 1) * 512], ALU.mult)
                        b.wait("vector", pe, h3_cnt[e_])
                        if e_ >= 2:
                            b.wait("vector", pe, mixu_cnt[e_ - 2])
                        act_cnt[e_] = b.tt(wactb[e_ % 2][:, :], s1b[e_ % 2][:, :],
                                           h3b[:, :], ALU.mult, ctr=ve)
                        wact_cnt[e_] = act_cnt[e_]
                        if e_ >= 1:
                            emit_mixu(e_ - 1)
                    emit_mixu(NE - 1)
                    first_unit = False
                    prev_silu = silu_cnt
                    prev_act = act_cnt
                    prev_wact7 = wact_cnt[7]
                    freeT0_se = silu_cnt[7]
                    freeT1_ve = act_cnt[7]
                    base1_gate_pe = h1c
                    base3_gate_pe = h3_cnt[7]
                    b.wait("scalar", pe, mixu_cnt[7])
                    b.wait("scalar", pe, mixbuf_gate_pe)
                    mix_start_se = b.scopy(mix[:, Fg * 512:(Fg + 1) * 512], MIXB[:, :],
                                           ctr=se)
                    mixevac_last = mix_start_se
                lora_free = pe.n
            b.wait("scalar", pe, pe.n)
            for j in range(3):
                b.scopy(u2w[:, j * 512:(j + 1) * 512], U2[j][:, :], scale=2.0, ctr=se)
            u2w_done = se.n
            u2_memset_gate_se = u2w_done

            for dt in range(NDT):
                bi = dt % 2
                b.wait("gpsimd", pe, w2_free[bi])
                b.dma("gpsimd", gd, w2st[bi].rearrange("p (ft m) -> p ft m", ft=NFT),
                      w2Tb[dt])
                w2ld = gd.n
                b.wait("sync", od, prev_store[bi])
                b.dma("sync", sd, x2ch[bi][:, :],
                      x2s[dt * 128:(dt + 1) * 128, th * 512:(th + 1) * 512])
                x2ld = sd.n
                pb = T2 if dt % 2 == 0 else T3
                b.wait("tensor", gd, w2ld)
                b.wait("tensor", se, mixevac_last)
                b.wait("tensor", se, u2w_done)
                b.wait("tensor", ve, prev_out_ve[dt % 2])
                if dt == 0:
                    b.wait("tensor", se, prev_silu[6])
                    b.wait("tensor", ve, prev_act[6])
                for ftg in range(NFT):
                    b.mm(pb[:, :], w2st[bi][:, ftg * 128:(ftg + 1) * 128],
                         mix[:, ftg * 512:(ftg + 1) * 512],
                         start=(ftg == 0), stop=False)
                for j in range(3):
                    b.mm(pb[:, :], b2s[:, (j * NDT + dt) * 128:(j * NDT + dt + 1) * 128],
                         u2w[:, j * 512:(j + 1) * 512],
                         start=False, stop=(j == 2), ctr=pe if j == 2 else None)
                pw = pe.n
                b.wait("vector", pe, pw)
                b.wait("vector", sd, x2ld)
                ovo = b.tt(x2ch[bi][:, :], pb[:, :], x2ch[bi][:, :], ALU.add, ctr=ve)
                prev_out_ve[dt % 2] = ovo
                if dt % 2 == 0:
                    freeT2_ve = ovo
                else:
                    freeT3_ve2 = ovo
                b.wait("sync", ve, ovo)
                b.dma("sync", od, out_h[dt * 128:(dt + 1) * 128, th * 512:(th + 1) * 512],
                      x2ch[bi][:, :])
                prev_store[bi] = od.n
                w2_free[bi] = pw
            mixbuf_gate_pe = pe.n
            u2w_gate_pe = pe.n
            ln2th_free = pe.n
            uz_free = pe.n
            wb_gate_ve = ve.n
        st_K.close()
        st_H.close()

        b.wait("sync", od, od.n)
        assert od.n == 16 * 32, od.n

        nops = {k: len(v) for k, v in b.ops.items()}
        print("kernel build: ops per engine:", nops, flush=True)

        with nc.Block() as block:
            @block.sync
            def _(e):
                for f in b.ops["sync"]:
                    f(e)

            @block.gpsimd
            def _(e):
                for f in b.ops["gpsimd"]:
                    f(e)

            @block.tensor
            def _(e):
                for f in b.ops["tensor"]:
                    f(e)

            @block.scalar
            def _(e):
                for f in b.ops["scalar"]:
                    f(e)

            @block.vector
            def _(e):
                for f in b.ops["vector"]:
                    f(e)

    return nc


_PROG_CACHE = {}


def kernel(**inputs):
    x = np.ascontiguousarray(np.asarray(inputs["L0_attn_x"], dtype=np.float32))
    wq = np.asarray(inputs["L0_attn_wq"], dtype=np.float32)
    wk = np.asarray(inputs["L0_attn_wk"], dtype=np.float32)
    wv = np.asarray(inputs["L0_attn_wv"], dtype=np.float32)
    wo = np.asarray(inputs["L0_attn_wo"], dtype=np.float32)
    bo = np.asarray(inputs["L0_attn_bo"], dtype=np.float32)
    gate_w = np.asarray(inputs["L0_ffn_gate_w"], dtype=np.float32)
    w1 = np.asarray(inputs["L0_ffn_w1"], dtype=np.float32)
    w3 = np.asarray(inputs["L0_ffn_w3"], dtype=np.float32)
    w2 = np.asarray(inputs["L0_ffn_w2"], dtype=np.float32)
    a1 = np.asarray(inputs["L0_ffn_a1"], dtype=np.float32)
    b1 = np.asarray(inputs["L0_ffn_b1"], dtype=np.float32)
    a3 = np.asarray(inputs["L0_ffn_a3"], dtype=np.float32)
    b3 = np.asarray(inputs["L0_ffn_b3"], dtype=np.float32)
    a2 = np.asarray(inputs["L0_ffn_a2"], dtype=np.float32)
    b2 = np.asarray(inputs["L0_ffn_b2"], dtype=np.float32)

    asc = np.ascontiguousarray
    shared = dict(
        wqTb=asc(wq.T.reshape(NDT, 128, 4, 512).transpose(2, 1, 0, 3)),
        wkTb=asc(wk.T.reshape(NDT, 128, 4, 512).transpose(2, 1, 0, 3)),
        wvTb=asc(wv.T.reshape(NDT, 128, 4, 512).transpose(2, 1, 0, 3)),
        woTb=asc(wo.T.reshape(NDT, 128, NDT, 128).transpose(2, 1, 0, 3)),
        bo1p=asc(bo.reshape(1, D)),
        gateT=asc(gate_w.T.reshape(NDT, 128, NE).transpose(1, 0, 2)),
        w1Tb=asc(w1.T.reshape(NDT, 128, NFT, 128).transpose(2, 1, 0, 3)),
        w3Tb=asc(w3.T.reshape(NDT, 128, NFT, 128).transpose(2, 1, 0, 3)),
        w2Tb=asc(w2.T.reshape(NFT, 128, NDT, 128).transpose(2, 1, 0, 3)),
        a1sT=asc(a1.reshape(NE * 16, D).T.reshape(NDT, 128, 128).transpose(1, 0, 2)),
        a3sT=asc(a3.reshape(NE * 16, D).T.reshape(NDT, 128, 128).transpose(1, 0, 2)),
        b1sT=asc(b1.transpose(0, 2, 1).reshape(NE * 16, FF)),
        b3sT=asc(b3.transpose(0, 2, 1).reshape(NE * 16, FF)),
        a2b=asc(np.stack([a2[e_].T.reshape(NFT, 128, 16).transpose(1, 0, 2)
                          for e_ in range(NE)])
                .reshape(NE, 128, 2, 22, 16).transpose(2, 0, 1, 3, 4)),
        identin=np.eye(128, dtype=np.float32),
        onescolin=np.ones((128, 1), np.float32),
        onesrowin=np.ones((1, 512), np.float32),
    )
    b2stk = np.zeros((3, 128, D), np.float32)
    for e_ in range(NE):
        b2stk[e_ // 3, 32 * (e_ % 3):32 * (e_ % 3) + 16, :] = b2[e_].T
    shared["b2stk"] = asc(b2stk)
    ii = np.arange(128)
    tri = np.zeros((128, 128), np.float32)
    tri[ii[:, None] > ii[None, :]] = NEG
    full = np.full((128, 128), NEG, np.float32)
    zer = np.zeros((128, 128), np.float32)
    shared["trimaskin"] = np.stack([np.concatenate([tri, zer], 1),
                                    np.concatenate([full, tri], 1)])
    sel = np.zeros((128, NE * 128), np.float32)
    for e_ in range(NE):
        sel[e_, e_ * 128:(e_ + 1) * 128] = 1.0
    shared["selin"] = sel

    full = np.full((128, 128), NEG, np.float32)
    zer = np.zeros((128, 128), np.float32)
    if "prog" not in _PROG_CACHE:
        _PROG_CACHE["prog"] = build_program()
    nc = _PROG_CACHE["prog"]

    in_maps = []
    orders = []
    for c in range(8):
        bb, p = c // 2, c % 2
        order = [2 * i + p for i in range(8)] + [2 * i + (1 - p) for i in range(8)]
        orders.append(order)
        xTb = asc(x[bb].T.reshape(D, 16, 128)[:, order, :].reshape(D, S))
        if p == 0:
            dm = np.stack([np.concatenate([full, zer], 1),
                           np.concatenate([full, full], 1)])
        else:
            dm = np.stack([np.zeros((128, 256), np.float32),
                           np.concatenate([full, zer], 1)])
        m = {"xT": xTb, "dmask": dm}
        m.update(shared)
        in_maps.append(m)

    res = run_bass_kernel_spmd(nc, in_maps, core_ids=list(range(8)))
    outp = np.empty((4, S, D), np.float32)
    for c in range(8):
        bb, p = c // 2, c % 2
        o = np.asarray(res.results[c]["out"])
        ot = np.ascontiguousarray(o.T)
        for i in range(8):
            g = orders[c][i]
            outp[bb, g * 128:(g + 1) * 128, :] = ot[i * 128:(i + 1) * 128, :]
    return outp


# version 16
# speedup vs baseline: 1.2056x; 1.2056x over previous
import numpy as np
from contextlib import ExitStack

import concourse.bass as bass
import concourse.mybir as mybir
from concourse.bass_utils import run_bass_kernel_spmd

F32 = mybir.dt.float32
BF16 = mybir.dt.bfloat16
AF = mybir.ActivationFunctionType
ALU = mybir.AluOpType
AX = mybir.AxisListType

D = 2048
S = 2048
H = 16
FF = 5632
NFT = 44
NE = 8
NDT = 16
TOWN = 1024
EPS = 1e-5
ISQ = 0.08838834764831845
NEG = -1.0e30
ENGS = ("sync", "gpsimd", "tensor", "scalar", "vector")


class Ctr:
    __slots__ = ("sem", "n")

    def __init__(self, sem):
        self.sem = sem
        self.n = 0


class Bld:
    def __init__(self, nc):
        self.nc = nc
        self.ops = {k: [] for k in ENGS}
        self.lastwait = {k: {} for k in ENGS}

    def emit(self, eng, fn):
        self.ops[eng].append(fn)

    def wait(self, eng, ctr, val):
        assert 0 <= val <= ctr.n, f"{eng} wait {val} > emitted {ctr.n}"
        if val <= 0:
            return
        lw = self.lastwait[eng]
        if lw.get(id(ctr), -1) >= val:
            return
        lw[id(ctr)] = val
        self.emit(eng, lambda e, s=ctr.sem, v=val: e.wait_ge(s, v))

    def dma(self, eng, ctr, dst, src):
        self.emit(eng, lambda e, d=dst, s=src, sem=ctr.sem:
                  e.dma_start(d, s).then_inc(sem, 16))
        ctr.n += 16
        return ctr.n

    def mm(self, out, lhsT, rhs, start, stop, ctr=None, sgc=False):
        if ctr is None:
            self.emit("tensor", lambda e, o=out, l=lhsT, r=rhs, a=start, b_=stop, g=sgc:
                      e.matmul(o, l, r, start=a, stop=b_, skip_group_check=g))
            return None
        self.emit("tensor", lambda e, o=out, l=lhsT, r=rhs, a=start, b_=stop, g=sgc, s=ctr.sem:
                  e.matmul(o, l, r, start=a, stop=b_, skip_group_check=g).then_inc(s, 1))
        ctr.n += 1
        return ctr.n

    def act(self, out, in_, func, ctr=None, **kw):
        if ctr is None:
            self.emit("scalar", lambda e, o=out, i=in_, f=func, k=kw:
                      e.activation(o, i, f, **k))
            return None
        self.emit("scalar", lambda e, o=out, i=in_, f=func, k=kw, s=ctr.sem:
                  e.activation(o, i, f, **k).then_inc(s, 1))
        ctr.n += 1
        return ctr.n

    def scopy(self, out, in_, scale=None, ctr=None):
        if scale is None:
            fn = lambda e, o=out, i=in_: e.copy(o, i)
        else:
            fn = lambda e, o=out, i=in_, m=scale: e.mul(o, i, m)
        if ctr is None:
            self.emit("scalar", fn)
            return None
        self.emit("scalar", lambda e, f=fn, s=ctr.sem: f(e).then_inc(s, 1))
        ctr.n += 1
        return ctr.n

    def tt(self, out, in0, in1, op, ctr=None):
        fn = lambda e, o=out, a=in0, b_=in1, p=op: e.tensor_tensor(o, a, b_, p)
        if ctr is None:
            self.emit("vector", fn)
            return None
        self.emit("vector", lambda e, f=fn, s=ctr.sem: f(e).then_inc(s, 1))
        ctr.n += 1
        return ctr.n

    def vop(self, fn, ctr=None):
        if ctr is None:
            self.emit("vector", fn)
            return None
        self.emit("vector", lambda e, f=fn, s=ctr.sem: f(e).then_inc(s, 1))
        ctr.n += 1
        return ctr.n


def build_program():
    nc = bass.Bass("TRN2", target_bir_lowering=False)

    xT = nc.dram_tensor("xT", [D, S], F32, kind="ExternalInput")
    dmaskin = nc.dram_tensor("dmask", [4, 128, 512], F32, kind="ExternalInput")
    wqTb = nc.dram_tensor("wqTb", [4, 128, NDT, 512], F32, kind="ExternalInput")
    wkTb = nc.dram_tensor("wkTb", [4, 128, NDT, 512], F32, kind="ExternalInput")
    wvTb = nc.dram_tensor("wvTb", [4, 128, NDT, 512], F32, kind="ExternalInput")
    woTb = nc.dram_tensor("woTb", [NDT, 128, NDT, 128], F32, kind="ExternalInput")
    bo1p = nc.dram_tensor("bo1p", [1, D], F32, kind="ExternalInput")
    gateT = nc.dram_tensor("gateT", [128, NDT, NE], F32, kind="ExternalInput")
    w1Tb = nc.dram_tensor("w1Tb", [NFT, 128, NDT, 128], F32, kind="ExternalInput")
    w3Tb = nc.dram_tensor("w3Tb", [NFT, 128, NDT, 128], F32, kind="ExternalInput")
    w2Tb = nc.dram_tensor("w2Tb", [NDT, 128, NFT, 128], F32, kind="ExternalInput")
    a1sT = nc.dram_tensor("a1sT", [128, NDT, 128], F32, kind="ExternalInput")
    a3sT = nc.dram_tensor("a3sT", [128, NDT, 128], F32, kind="ExternalInput")
    b1sT = nc.dram_tensor("b1sT", [128, FF], F32, kind="ExternalInput")
    b3sT = nc.dram_tensor("b3sT", [128, FF], F32, kind="ExternalInput")
    a2b = nc.dram_tensor("a2b", [NE, 128, NFT, 16], F32, kind="ExternalInput")
    b2stk = nc.dram_tensor("b2stk", [3, 128, D], F32, kind="ExternalInput")
    identin = nc.dram_tensor("identin", [128, 128], F32, kind="ExternalInput")
    onescolin = nc.dram_tensor("onescolin", [128, 1], F32, kind="ExternalInput")
    onesrowin = nc.dram_tensor("onesrowin", [1, 512], F32, kind="ExternalInput")
    trimaskin = nc.dram_tensor("trimaskin", [4, 128, 512], F32, kind="ExternalInput")
    selin = nc.dram_tensor("selin", [128, NE * 128], F32, kind="ExternalInput")
    out_h = nc.dram_tensor("out", [D, TOWN], F32, kind="ExternalOutput")
    x2s = nc.dram_tensor("x2s", [D, TOWN], F32)
    ln2s = nc.dram_tensor("ln2s", [D, TOWN], BF16)

    gctx = ExitStack()
    with gctx:
        E = gctx.enter_context
        b = Bld(nc)
        sd = Ctr(E(nc.semaphore("sd")))
        gd = Ctr(E(nc.semaphore("gd")))
        od = Ctr(E(nc.semaphore("od")))
        pe = Ctr(E(nc.semaphore("pe")))
        ve = Ctr(E(nc.semaphore("ve")))
        se = Ctr(E(nc.semaphore("se")))
        PB = [E(nc.psum_tensor(f"pb{i}", [128, 512], F32)) for i in range(8)]

        I_bf = E(nc.sbuf_tensor("I_bf", [128, 128], BF16))
        I_f = E(nc.sbuf_tensor("I_f", [128, 128], F32))
        onescol_bf = E(nc.sbuf_tensor("onescol_bf", [128, 1], BF16))
        onesrow = E(nc.sbuf_tensor("onesrow", [1, 512], F32))
        onesrow_bf = E(nc.sbuf_tensor("onesrow_bf", [1, 128], BF16))
        sel_sb = E(nc.sbuf_tensor("sel_sb", [128, NE * 128], BF16))

        b.dma("gpsimd", gd, I_bf[:, :], identin[:, :])
        b.dma("gpsimd", gd, onescol_bf[:, :], onescolin[:, :])
        b.dma("gpsimd", gd, onesrow_bf[:, :], onesrowin[0:1, 0:128])
        b.dma("sync", sd, I_f[:, :], identin[:, :])
        b.dma("sync", sd, onesrow[:, :], onesrowin[:, :])
        b.dma("gpsimd", gd, sel_sb[:, :], selin[:, :])
        consts_gd = gd.n
        consts_sd = sd.n
        b.wait("tensor", gd, consts_gd)
        b.wait("tensor", sd, consts_sd)
        b.wait("vector", sd, consts_sd)
        b.wait("scalar", sd, consts_sd)

        arena = E(nc.sbuf_tensor("arena", [128, H * TOWN], BF16))
        attn_out = arena

        st_A = ExitStack()
        ln1 = st_A.enter_context(nc.sbuf_tensor("ln1", [128, NDT * S], BF16))
        st_B = ExitStack()
        E1 = st_B.enter_context
        xbf = [E1(nc.sbuf_tensor(f"xbf{i}", [128, S], BF16)) for i in range(2)]
        sqbf = [E1(nc.sbuf_tensor(f"sqbf{i}", [128, S], BF16)) for i in range(2)]
        rowm = E1(nc.sbuf_tensor("rowm", [1, S], F32))
        rowa = E1(nc.sbuf_tensor("rowa", [1, S], F32))
        rowb = E1(nc.sbuf_tensor("rowb", [1, S], F32))
        mb1 = E1(nc.sbuf_tensor("mb1", [128, S], F32))
        rb1 = E1(nc.sbuf_tensor("rb1", [128, S], F32))
        xf = [E1(nc.sbuf_tensor(f"xf{i}", [128, S], F32)) for i in range(2)]

        sq_done = [0, 0]
        mm_done = [0, 0]
        for dt in range(NDT):
            bi = dt % 2
            if dt >= 2:
                b.wait("gpsimd", ve, sq_done[bi])
                b.wait("gpsimd", pe, mm_done[bi])
            b.dma("gpsimd", gd, xbf[bi][:, :], xT[dt * 128:(dt + 1) * 128, :])
            b.wait("vector", gd, gd.n)
            if dt >= 2:
                b.wait("vector", pe, mm_done[bi])
            sq_done[bi] = b.tt(sqbf[bi][:, :], xbf[bi][:, :], xbf[bi][:, :], ALU.mult, ctr=ve)
            b.wait("tensor", ve, sq_done[bi])
            for c in range(4):
                b.mm(PB[c][0:1, :], onescol_bf[:, :], xbf[bi][:, c * 512:(c + 1) * 512],
                     start=(dt == 0), stop=(dt == NDT - 1))
                b.mm(PB[4 + c][0:1, :], onescol_bf[:, :], sqbf[bi][:, c * 512:(c + 1) * 512],
                     start=(dt == 0), stop=(dt == NDT - 1),
                     ctr=pe if c == 3 else None)
            mm_done[bi] = pe.n
        b.wait("scalar", pe, pe.n)
        for c in range(4):
            b.scopy(rowm[0:1, c * 512:(c + 1) * 512], PB[c][0:1, :], scale=1.0 / D)
            b.scopy(rowa[0:1, c * 512:(c + 1) * 512], PB[4 + c][0:1, :],
                    scale=1.0 / D, ctr=se)
        b.wait("vector", se, se.n)
        c = b.tt(rowb[0:1, :], rowm[0:1, :], rowm[0:1, :], ALU.mult, ctr=ve)
        b.wait("vector", ve, c)
        c = b.tt(rowa[0:1, :], rowa[0:1, :], rowb[0:1, :], ALU.subtract, ctr=ve)
        b.wait("vector", ve, c)
        b.vop(lambda e, o=rowa[0:1, :]: e.tensor_scalar(o, o, EPS, None, op0=ALU.add),
              ctr=ve)
        b.wait("scalar", ve, ve.n)
        b.act(rowb[0:1, :], rowa[0:1, :], AF.Ln)
        b.act(rowa[0:1, :], rowb[0:1, :], AF.Exp, scale=-0.5, ctr=se)
        b.wait("tensor", se, se.n)
        for c in range(4):
            b.mm(PB[c][:, :], onesrow[0:1, 0:128], rowm[0:1, c * 512:(c + 1) * 512],
                 start=True, stop=True)
            b.mm(PB[4 + c][:, :], onesrow[0:1, 0:128], rowa[0:1, c * 512:(c + 1) * 512],
                 start=True, stop=True, ctr=pe)
        b.wait("scalar", pe, pe.n)
        for c in range(4):
            b.scopy(mb1[:, c * 512:(c + 1) * 512], PB[c][:, :])
            b.scopy(rb1[:, c * 512:(c + 1) * 512], PB[4 + c][:, :], ctr=se)
        bc1_done = se.n
        ap_done = [0, 0]
        for dt in range(NDT):
            bi = dt % 2
            if dt >= 2:
                b.wait("sync", ve, ap_done[bi])
            b.dma("sync", sd, xf[bi][:, :], xT[dt * 128:(dt + 1) * 128, :])
            b.wait("vector", sd, sd.n)
            b.wait("vector", se, bc1_done)
            b.tt(xf[bi][:, :], xf[bi][:, :], mb1[:, :], ALU.subtract)
            ap_done[bi] = b.tt(ln1[:, dt * S:(dt + 1) * S], xf[bi][:, :], rb1[:, :],
                               ALU.mult, ctr=ve)
        ln1_done = ve.n
        st_B.close()

        qt_units = []
        for i in range(2):
            us = []
            for kt in range(4 * i + 4):
                us.append((kt, ("t", kt - 4 * i) if kt >= 4 * i else None))
            for j in range(4 * i + 4):
                us.append((8 + j, ("d", j - 4 * i) if j >= 4 * i else None))
            qt_units.append(us)

        st_C = ExitStack()
        E2 = st_C.enter_context
        wbuf = [E2(nc.sbuf_tensor(f"wbuf{i}", [128, NDT * 512], BF16)) for i in range(2)]
        k_bf = E2(nc.sbuf_tensor("k_bf", [128, 4 * S], BF16))
        v_sb = E2(nc.sbuf_tensor("v_sb", [128, 16 * 512], BF16))
        q_bf = E2(nc.sbuf_tensor("q_bf", [128, 4 * TOWN], BF16))
        ebuf = [E2(nc.sbuf_tensor(f"ebuf{i}", [128, 512], BF16)) for i in range(2)]
        r_sb = E2(nc.sbuf_tensor("r_sb", [1, 512], F32))
        r_bf = E2(nc.sbuf_tensor("r_bf", [1, 512], BF16))
        bc_sb = [E2(nc.sbuf_tensor(f"bc_sb{i}", [128, 512], F32)) for i in range(2)]
        trimask_sb = E2(nc.sbuf_tensor("trimask_sb", [128, 4 * 512], F32))
        dmask_sb = E2(nc.sbuf_tensor("dmask_sb", [128, 4 * 512], F32))
        b.dma("sync", sd, trimask_sb.rearrange("p (j m) -> p j m", j=4),
              trimaskin.rearrange("j p m -> p j m"))
        b.dma("sync", sd, dmask_sb.rearrange("p (j m) -> p j m", j=4),
              dmaskin.rearrange("j p m -> p j m"))
        b.wait("vector", sd, sd.n)

        wbuf_free = [0, 0]
        bank_ev = [0, 0]
        prev_att = [0, 0]
        prev_rec = [0, 0]
        prev_bcc = [0, 0]
        sc_cons = [0, 0]
        prev_avd_last = 0
        prev_bmn = 0
        i_hq = 0
        for g in range(4):
            for wi, (wsrc, kind) in enumerate(((wkTb, "k"), (wvTb, "v"), (wqTb, "q"))):
                bi = (g * 3 + wi) % 2
                b.wait("gpsimd", pe, wbuf_free[bi])
                b.wait("gpsimd", ve, ln1_done)
                b.dma("gpsimd", gd,
                      wbuf[bi].rearrange("p (dt m) -> p dt m", dt=NDT),
                      wsrc[g])
                loaded = gd.n
                b.wait("tensor", gd, loaded)
                b.wait("tensor", ve, ln1_done)
                if kind == "k":
                    for mt in range(4):
                        for tc in range(4):
                            bk = (mt * 4 + tc) % 2
                            b.wait("tensor", se, bank_ev[bk])
                            for dt in range(NDT):
                                b.mm(PB[bk][:, :],
                                     wbuf[bi][:, dt * 512 + mt * 128:dt * 512 + (mt + 1) * 128],
                                     ln1[:, dt * S + tc * 512:dt * S + (tc + 1) * 512],
                                     start=(dt == 0), stop=(dt == NDT - 1),
                                     ctr=pe if dt == NDT - 1 else None)
                            b.wait("scalar", pe, pe.n)
                            bank_ev[bk] = b.scopy(
                                k_bf[:, mt * S + tc * 512:mt * S + (tc + 1) * 512],
                                PB[bk][:, :], ctr=se)
                elif kind == "v":
                    for tt in range(16):
                        bk = tt % 2
                        b.wait("tensor", se, bank_ev[bk])
                        for dt in range(NDT):
                            b.mm(PB[bk][:, :],
                                 ln1[:, dt * S + tt * 128:dt * S + (tt + 1) * 128],
                                 wbuf[bi][:, dt * 512:(dt + 1) * 512],
                                 start=(dt == 0), stop=(dt == NDT - 1),
                                 ctr=pe if dt == NDT - 1 else None)
                        b.wait("scalar", pe, pe.n)
                        bank_ev[bk] = b.scopy(v_sb[:, tt * 512:(tt + 1) * 512],
                                              PB[bk][:, :], ctr=se)
                else:
                    for mt in range(4):
                        for tc in range(2):
                            bk = (mt * 2 + tc) % 2
                            b.wait("tensor", se, bank_ev[bk])
                            for dt in range(NDT):
                                b.mm(PB[bk][:, :],
                                     wbuf[bi][:, dt * 512 + mt * 128:dt * 512 + (mt + 1) * 128],
                                     ln1[:, dt * S + tc * 512:dt * S + (tc + 1) * 512],
                                     start=(dt == 0), stop=(dt == NDT - 1),
                                     ctr=pe if dt == NDT - 1 else None)
                            b.wait("scalar", pe, pe.n)
                            bank_ev[bk] = b.scopy(
                                q_bf[:, mt * TOWN + tc * 512:mt * TOWN + (tc + 1) * 512],
                                PB[bk][:, :], ctr=se)
                wbuf_free[bi] = pe.n
            qkv_ev = se.n
            sc_cons = [max(sc_cons[0], bank_ev[0]), max(sc_cons[1], bank_ev[1])]
            for hl in range(4):
                hglob = g * 4 + hl
                for pi in range(2):
                    units = qt_units[pi]
                    U = len(units)
                    ai = i_hq % 2
                    avb = PB[2 + ai]
                    dnb = PB[6 + ai]
                    bcb = PB[4 + ai]
                    exp_cnt = [0] * U
                    scr_cnt = [0] * U
                    avd_cnt = [0] * U
                    def emit_avd(w):
                        ktw, _ = units[w]
                        if w == 0:
                            b.wait("tensor", ve, prev_att[ai])
                            b.wait("tensor", ve, prev_rec[ai])
                        b.wait("tensor", se, exp_cnt[w])
                        b.mm(avb[:, :],
                             v_sb[:, ktw * 512 + hl * 128:ktw * 512 + (hl + 1) * 128],
                             ebuf[w % 2][:, :],
                             start=(w == 0), stop=(w == U - 1))
                        avd_cnt[w] = b.mm(dnb[0:1, :], onescol_bf[:, :],
                                          ebuf[w % 2][:, :],
                                          start=(w == 0), stop=(w == U - 1), ctr=pe)
                    mv = 0
                    for u in range(U):
                        kt, mk = units[u]
                        bank = PB[u % 2]
                        b.wait("tensor", se, sc_cons[u % 2])
                        b.wait("tensor", se, qkv_ev)
                        scr_cnt[u] = b.mm(
                            bank[:, :],
                            k_bf[:, hl * S + kt * 128:hl * S + (kt + 1) * 128],
                            q_bf[:, hl * TOWN + pi * 512:hl * TOWN + (pi + 1) * 512],
                            start=True, stop=True, ctr=pe)
                        if mk is not None:
                            msk = trimask_sb if mk[0] == "t" else dmask_sb
                            moff = mk[1] * 512
                            b.wait("vector", pe, scr_cnt[u])
                            mv = b.tt(bank[:, :], bank[:, :],
                                      msk[:, moff:moff + 512], ALU.add, ctr=ve)
                        b.wait("scalar", pe, scr_cnt[u])
                        b.wait("scalar", ve, mv)
                        b.wait("scalar", pe,
                               avd_cnt[u - 2] if u >= 2 else prev_avd_last)
                        exp_cnt[u] = b.act(ebuf[u % 2][:, :], bank[:, :],
                                           AF.Exp, scale=ISQ, ctr=se)
                        sc_cons[u % 2] = exp_cnt[u]
                        if u >= 1:
                            emit_avd(u - 1)
                    emit_avd(U - 1)
                    prev_avd_last = avd_cnt[U - 1]
                    b.wait("vector", pe, avd_cnt[U - 1])
                    prev_rec[ai] = b.vop(
                        lambda e, o=r_sb[0:1, :], i=dnb[0:1, :]: e.reciprocal(o, i),
                        ctr=ve)
                    b.wait("scalar", ve, prev_rec[ai])
                    b.wait("scalar", pe, prev_bmn)
                    rc = b.scopy(r_bf[0:1, :], r_sb[0:1, :], ctr=se)
                    b.wait("tensor", se, rc)
                    b.wait("tensor", se, prev_bcc[ai])
                    bmn = b.mm(bcb[:, :], onesrow_bf[0:1, :], r_bf[0:1, :],
                               start=True, stop=True, ctr=pe)
                    prev_bmn = bmn
                    b.wait("scalar", pe, bmn)
                    prev_bcc[ai] = b.scopy(bc_sb[ai][:, :], bcb[:, :], ctr=se)
                    b.wait("vector", se, prev_bcc[ai])
                    prev_att[ai] = b.tt(
                        attn_out[:, hglob * TOWN + pi * 512:hglob * TOWN + (pi + 1) * 512],
                        avb[:, :], bc_sb[ai][:, :], ALU.mult, ctr=ve)
                    i_hq += 1
        att_done = ve.n
        att_pe = pe.n
        att_se = se.n
        st_C.close()
        st_A.close()

        st_H = ExitStack()
        EH = st_H.enter_context
        wfm = EH(nc.sbuf_tensor("wfm", [128, TOWN], BF16))
        wb_bf = EH(nc.sbuf_tensor("wb_bf", [128, NE * 512], BF16))
        u1_sb = EH(nc.sbuf_tensor("u1_sb", [128, TOWN], BF16))
        u3_sb = EH(nc.sbuf_tensor("u3_sb", [128, TOWN], BF16))
        st_E = ExitStack()
        x2_sb = st_E.enter_context(nc.sbuf_tensor("x2_sb", [128, NDT * TOWN], F32))
        st_D = ExitStack()
        E3 = st_D.enter_context
        wobuf = [E3(nc.sbuf_tensor(f"wobuf{i}", [128, NDT * 128], BF16)) for i in range(2)]
        x_own = E3(nc.sbuf_tensor("x_own", [128, NDT * TOWN], F32))
        bo_sb = E3(nc.sbuf_tensor("bo_sb", [1, D], F32))

        b.wait("sync", pe, att_pe)
        b.wait("sync", ve, att_done)
        b.wait("sync", se, att_se)
        b.dma("sync", sd, bo_sb[:, :], bo1p[:, :])
        for dt in range(NDT):
            b.dma("sync", sd, x_own[:, dt * TOWN:(dt + 1) * TOWN],
                  xT[dt * 128:(dt + 1) * 128, 0:TOWN])
        xown_ld = sd.n
        wo_free = [0, 0]
        wo_bank = [att_done, att_done]
        x2done = 0
        for dt in range(NDT):
            bi = dt % 2
            b.wait("gpsimd", pe, wo_free[bi])
            b.wait("gpsimd", ve, att_done)
            b.dma("gpsimd", gd,
                  wobuf[bi].rearrange("p (kt m) -> p kt m", kt=NDT),
                  woTb[dt])
            lded = gd.n
            b.wait("tensor", gd, lded)
            b.wait("tensor", ve, att_done)
            b.wait("tensor", se, att_se)
            for tc in range(2):
                bk = (dt * 2 + tc) % 2
                b.wait("tensor", ve, wo_bank[bk])
                for kt in range(NDT):
                    b.mm(PB[bk][:, :],
                         wobuf[bi][:, kt * 128:(kt + 1) * 128],
                         attn_out[:, kt * TOWN + tc * 512:kt * TOWN + (tc + 1) * 512],
                         start=(kt == 0), stop=False)
                pc = b.mm(PB[bk][:, :], bo_sb[0:1, dt * 128:(dt + 1) * 128],
                          onesrow[0:1, :], start=False, stop=True, ctr=pe)
                b.wait("vector", pe, pc)
                b.wait("vector", sd, xown_ld)
                x2done = b.tt(
                    x2_sb[:, dt * TOWN + tc * 512:dt * TOWN + (tc + 1) * 512],
                    PB[bk][:, :],
                    x_own[:, dt * TOWN + tc * 512:dt * TOWN + (tc + 1) * 512],
                    ALU.add, ctr=ve)
                wo_bank[bk] = x2done
            wo_free[bi] = pe.n
        x2_done = ve.n
        wo_pe = pe.n
        st_D.close()

        st_G = ExitStack()
        ln2 = st_G.enter_context(nc.sbuf_tensor("ln2", [128, NDT * TOWN], BF16))
        st_F = ExitStack()
        E4 = st_F.enter_context
        x2bf = E4(nc.sbuf_tensor("x2bf", [128, TOWN], BF16))
        sq2 = E4(nc.sbuf_tensor("sq2", [128, TOWN], BF16))
        rowm2 = E4(nc.sbuf_tensor("rowm2", [1, TOWN], F32))
        rowa2 = E4(nc.sbuf_tensor("rowa2", [1, TOWN], F32))
        rowb2 = E4(nc.sbuf_tensor("rowb2", [1, TOWN], F32))
        mb2 = E4(nc.sbuf_tensor("mb2", [128, TOWN], F32))
        rb2 = E4(nc.sbuf_tensor("rb2", [128, TOWN], F32))

        b.wait("sync", ve, x2_done)
        x2sp = b.dma("sync", sd, x2s.rearrange("(dt p) t -> p dt t", p=128),
                     x2_sb.rearrange("p (dt t) -> p dt t", dt=NDT))

        mm2_done = 0
        for dt in range(NDT):
            sl = x2_sb[:, dt * TOWN:(dt + 1) * TOWN]
            b.wait("vector", ve, x2_done)
            if dt >= 1:
                b.wait("vector", pe, mm2_done)
            b.vop(lambda e, o=x2bf[:, :], i=sl: e.tensor_copy(o, i))
            sqd = b.tt(sq2[:, :], sl, sl, ALU.mult, ctr=ve)
            b.wait("tensor", ve, sqd)
            b.wait("tensor", ve, x2_done)
            b.wait("tensor", se, att_se)
            for c in range(2):
                b.mm(PB[c][0:1, :], onescol_bf[:, :], x2bf[:, c * 512:(c + 1) * 512],
                     start=(dt == 0), stop=(dt == NDT - 1))
                b.mm(PB[2 + c][0:1, :], onescol_bf[:, :], sq2[:, c * 512:(c + 1) * 512],
                     start=(dt == 0), stop=(dt == NDT - 1),
                     ctr=pe if c == 1 else None)
            mm2_done = pe.n
        b.wait("scalar", pe, mm2_done)
        for c in range(2):
            b.scopy(rowm2[0:1, c * 512:(c + 1) * 512], PB[c][0:1, :], scale=1.0 / D)
            b.scopy(rowa2[0:1, c * 512:(c + 1) * 512], PB[2 + c][0:1, :],
                    scale=1.0 / D, ctr=se)
        b.wait("vector", se, se.n)
        c = b.tt(rowb2[0:1, :], rowm2[0:1, :], rowm2[0:1, :], ALU.mult, ctr=ve)
        b.wait("vector", ve, c)
        c = b.tt(rowa2[0:1, :], rowa2[0:1, :], rowb2[0:1, :], ALU.subtract, ctr=ve)
        b.wait("vector", ve, c)
        b.vop(lambda e, o=rowa2[0:1, :]: e.tensor_scalar(o, o, EPS, None, op0=ALU.add),
              ctr=ve)
        b.wait("scalar", ve, ve.n)
        b.act(rowb2[0:1, :], rowa2[0:1, :], AF.Ln)
        b.act(rowa2[0:1, :], rowb2[0:1, :], AF.Exp, scale=-0.5, ctr=se)
        b.wait("tensor", se, se.n)
        for c in range(2):
            b.mm(PB[c][:, :], onesrow[0:1, 0:128], rowm2[0:1, c * 512:(c + 1) * 512],
                 start=True, stop=True)
            b.mm(PB[2 + c][:, :], onesrow[0:1, 0:128], rowa2[0:1, c * 512:(c + 1) * 512],
                 start=True, stop=True, ctr=pe)
        b.wait("scalar", pe, pe.n)
        for c in range(2):
            b.scopy(mb2[:, c * 512:(c + 1) * 512], PB[c][:, :])
            b.scopy(rb2[:, c * 512:(c + 1) * 512], PB[2 + c][:, :], ctr=se)
        bc2_done = se.n
        ln2_done = 0
        b.wait("vector", se, bc2_done)
        b.wait("vector", sd, x2sp)
        for dt in range(NDT):
            sl = x2_sb[:, dt * TOWN:(dt + 1) * TOWN]
            b.tt(sl, sl, mb2[:, :], ALU.subtract)
            ln2_done = b.tt(ln2[:, dt * TOWN:(dt + 1) * TOWN], sl, rb2[:, :],
                            ALU.mult, ctr=ve)
        b.wait("sync", ve, ln2_done)
        ln2sp = b.dma("sync", sd, ln2s.rearrange("(dt p) t -> p dt t", p=128),
                      ln2.rearrange("p (dt t) -> p dt t", dt=NDT))
        st_F.close()

        st_I = ExitStack()
        E5 = st_I.enter_context
        gate_bf = E5(nc.sbuf_tensor("gate_bf", [128, NDT * NE], BF16))
        logit = E5(nc.sbuf_tensor("logit", [128, 8 * NE], F32))
        grows = E5(nc.sbuf_tensor("grows", [128, 8], F32))
        gtmp = E5(nc.sbuf_tensor("gtmp", [128, 4 * NE], F32))

        b.wait("gpsimd", ve, ln2_done)
        b.wait("gpsimd", sd, x2sp)
        b.dma("gpsimd", gd, gate_bf.rearrange("p (dt e) -> p dt e", dt=NDT),
              gateT[:, :, :])
        gld = gd.n
        zr = b.vop(lambda e, o=wfm[:, :]: e.memset(o, 0.0), ctr=ve)
        b.wait("tensor", gd, gld)
        b.wait("tensor", ve, ln2_done)
        b.wait("tensor", se, se.n)
        gevac = [0, 0]
        for tt in range(8):
            bk = 4 + tt % 2
            b.wait("tensor", se, gevac[tt % 2])
            for dt in range(NDT):
                b.mm(PB[bk][:, 0:8],
                     ln2[:, dt * TOWN + tt * 128:dt * TOWN + (tt + 1) * 128],
                     gate_bf[:, dt * NE:(dt + 1) * NE],
                     start=(dt == 0), stop=(dt == NDT - 1),
                     ctr=pe if dt == NDT - 1 else None)
            b.wait("scalar", pe, pe.n)
            gevac[tt % 2] = b.scopy(logit[:, tt * NE:(tt + 1) * NE], PB[bk][:, 0:8], ctr=se)
        b.wait("vector", se, se.n)
        wfm_evac = 0
        for tt in range(8):
            lg = logit[:, tt * NE:(tt + 1) * NE]
            msd = gtmp[:, 2 * NE:3 * NE]
            wtm = gtmp[:, 3 * NE:4 * NE]
            eq1 = gtmp[:, 0:NE]
            eq2 = gtmp[:, NE:2 * NE]
            m1 = grows[:, 0:1]
            m2 = grows[:, 1:2]
            dd = grows[:, 2:3]
            dn = grows[:, 3:4]
            r1 = grows[:, 4:5]
            r2 = grows[:, 5:6]
            negc = grows[:, 6:7]
            c = b.vop(lambda e, o=m1, i=lg: e.reduce_max(o, i, axis=AX.X), ctr=ve)
            b.wait("vector", ve, c)
            c = b.vop(lambda e, o=eq1, i=lg, s=m1:
                      e.tensor_tensor(o, i, s.broadcast_to([128, NE]), ALU.is_equal),
                      ctr=ve)
            b.wait("vector", ve, c)
            c = b.vop(lambda e, o=msd, i=eq1, s=lg:
                      e.scalar_tensor_tensor(o, i, NEG, s, op0=ALU.mult, op1=ALU.add),
                      ctr=ve)
            b.wait("vector", ve, c)
            c = b.vop(lambda e, o=m2, i=msd: e.reduce_max(o, i, axis=AX.X), ctr=ve)
            b.wait("vector", ve, c)
            c = b.vop(lambda e, o=eq2, i=msd, s=m2:
                      e.tensor_tensor(o, i, s.broadcast_to([128, NE]), ALU.is_equal),
                      ctr=ve)
            b.wait("vector", ve, c)
            c = b.tt(dd, m1, m2, ALU.subtract, ctr=ve)
            b.wait("vector", ve, c)
            dv = b.vop(lambda e, o=dn, i=dd:
                       e.tensor_scalar(o, i, -1.0, None, op0=ALU.mult), ctr=ve)
            b.wait("vector", ve, dv)
            b.wait("scalar", ve, dv)
            b.act(r1, dd, AF.Sigmoid)
            sg = b.act(r2, dn, AF.Sigmoid, ctr=se)
            b.wait("vector", se, sg)
            c = b.vop(lambda e, o=wtm, i=eq1, s=r1:
                      e.tensor_scalar(o, i, s, None, op0=ALU.mult), ctr=ve)
            b.wait("vector", ve, c)
            wv = b.vop(lambda e, o=wtm, i=eq2, s=r2, w=wtm:
                       e.scalar_tensor_tensor(o, i, s, w, op0=ALU.mult, op1=ALU.add),
                       ctr=ve)
            b.wait("vector", ve, wv)
            b.wait("tensor", ve, wv)
            b.wait("tensor", ve, zr)
            b.wait("tensor", se, wfm_evac)
            tm = b.mm(PB[6][0:8, 0:128], wtm, I_f[:, :], start=True, stop=True, ctr=pe)
            b.wait("scalar", pe, tm)
            wfm_evac = b.scopy(wfm[0:8, tt * 128:(tt + 1) * 128], PB[6][0:8, 0:128],
                               ctr=se)
        gate_done_se = wfm_evac
        st_I.close()

        st_J = ExitStack()
        E6a = st_J.enter_context
        a1buf = E6a(nc.sbuf_tensor("a1buf", [128, NDT * 128], BF16))
        a3buf = E6a(nc.sbuf_tensor("a3buf", [128, NDT * 128], BF16))
        b.wait("gpsimd", ve, ve.n)
        b.wait("gpsimd", pe, pe.n)
        b.dma("gpsimd", gd, a1buf.rearrange("p (dt m) -> p dt m", dt=NDT),
              a1sT[:, :, :])
        b.dma("gpsimd", gd, a3buf.rearrange("p (dt m) -> p dt m", dt=NDT),
              a3sT[:, :, :])
        b.wait("tensor", gd, gd.n)
        b.wait("tensor", se, gate_done_se)
        for (abuf, usb) in ((a1buf, u1_sb), (a3buf, u3_sb)):
            for tc in range(2):
                bk = 4 + tc
                for dt in range(NDT):
                    b.mm(PB[bk][:, :], abuf[:, dt * 128:(dt + 1) * 128],
                         ln2[:, dt * TOWN + tc * 512:dt * TOWN + (tc + 1) * 512],
                         start=(dt == 0), stop=(dt == NDT - 1),
                         ctr=pe if dt == NDT - 1 else None)
                b.wait("scalar", pe, pe.n)
                b.scopy(usb[:, tc * 512:(tc + 1) * 512], PB[bk][:, :], scale=2.0, ctr=se)
        u_done = se.n
        u_pe = pe.n
        st_J.close()
        st_G.close()
        st_E.close()

        st_K = ExitStack()
        EK = st_K.enter_context
        b1h = EK(nc.sbuf_tensor("b1h", [128, FF], BF16))
        b3h = EK(nc.sbuf_tensor("b3h", [128, FF], BF16))
        a2h = EK(nc.sbuf_tensor("a2h", [128, NFT * NE * 16], BF16))
        b2s = EK(nc.sbuf_tensor("b2s", [128, 3 * D], BF16))
        base1_bf = EK(nc.sbuf_tensor("base1_bf", [128, 512], BF16))
        base3_bf = EK(nc.sbuf_tensor("base3_bf", [128, 512], BF16))
        s1b = [EK(nc.sbuf_tensor(f"s1b{i}", [128, 512], BF16)) for i in range(2)]
        wactb = [EK(nc.sbuf_tensor(f"wactb{i}", [128, 512], BF16)) for i in range(2)]
        w1b = [EK(nc.sbuf_tensor(f"w1b{i}", [128, NDT * 128], BF16)) for i in range(2)]
        w3b = [EK(nc.sbuf_tensor(f"w3b{i}", [128, NDT * 128], BF16)) for i in range(2)]
        mix = EK(nc.sbuf_tensor("mix", [128, NFT * 512], BF16))
        u2w = EK(nc.sbuf_tensor("u2w", [128, 3 * 512], BF16))
        w2st = [EK(nc.sbuf_tensor(f"w2st{i}", [128, NFT * 128], BF16)) for i in range(2)]
        x2ch = [EK(nc.sbuf_tensor(f"x2ch{i}", [128, 512], F32)) for i in range(2)]

        b.wait("gpsimd", se, u_done)
        b.wait("gpsimd", pe, u_pe)
        b.wait("gpsimd", sd, ln2sp)
        b.wait("sync", pe, u_pe)
        b.wait("vector", pe, u_pe)
        b.dma("gpsimd", gd, b2s.rearrange("p (j m) -> p j m", j=3),
              b2stk.rearrange("j p m -> p j m"))
        b.dma("gpsimd", gd, b1h[:, :], b1sT[:, :])
        b.dma("gpsimd", gd, b3h[:, :], b3sT[:, :])
        a2h4 = a2h.rearrange("p (e f r) -> p e f r", e=NE, f=NFT)
        for e_ in range(NE):
            b.dma("gpsimd", gd, a2h4[:, e_, :, :], a2b[e_])
        b2s_ld = gd.n

        U2 = (PB[0], PB[1], PB[2])
        MIXB = PB[3]
        T0, T1, T2, T3 = PB[4], PB[5], PB[6], PB[7]

        w1_free = [0, 0]
        w2_free = [0, 0]
        lora_free = 0
        ln2th_free = 0
        uz_free = 0
        u2_memset_gate_se = u_done
        mix_start_se = u_done
        freeT0_se = u_done
        freeT1_ve = 0
        freeT2_se = u_done
        freeT3_ve = 0
        freeT2_ve = 0
        freeT3_ve2 = 0
        base1_gate_pe = 0
        base3_gate_pe = 0
        wb_gate_ve = 0
        u2w_gate_pe = 0
        mixbuf_gate_pe = 0
        prev_out_ve = [0, 0]
        prev_x2_ve = [0, 0]
        prev_store = [0, 0]
        mixevac_last = 0

        for th in range(2):
            b.wait("sync", pe, ln2th_free)
            b.wait("sync", sd, ln2sp)
            b.dma("sync", sd, arena[:, 0:8192].rearrange("p (dt t) -> p dt t", dt=NDT),
                  ln2s.rearrange("(dt p) t -> p dt t", p=128)[:, :, th * 512:(th + 1) * 512])
            ln2th_ld = sd.n

            b.wait("vector", se, u_done)
            b.wait("vector", pe, uz_free)
            b.vop(lambda e, o=arena[:, 8192:12288]: e.memset(o, 0.0))
            z3 = b.vop(lambda e, o=arena[:, 12288:16384]: e.memset(o, 0.0), ctr=ve)
            b.wait("gpsimd", ve, z3)
            for e_ in range(NE):
                b.dma("gpsimd", gd,
                      arena[e_ * 16:e_ * 16 + 16, 8192 + e_ * 512:8192 + (e_ + 1) * 512],
                      u1_sb[e_ * 16:e_ * 16 + 16, th * 512:(th + 1) * 512])
                b.dma("gpsimd", gd,
                      arena[e_ * 16:e_ * 16 + 16, 12288 + e_ * 512:12288 + (e_ + 1) * 512],
                      u3_sb[e_ * 16:e_ * 16 + 16, th * 512:(th + 1) * 512])
            uz_ld = gd.n

            b.wait("tensor", se, gate_done_se)
            b.wait("tensor", ve, wb_gate_ve)
            wbev = 0
            for e_ in range(NE):
                bk = T2 if e_ % 2 == 0 else T3
                b.wait("tensor", se, freeT2_se if e_ % 2 == 0 else u_done)
                b.wait("tensor", ve, freeT2_ve if e_ % 2 == 0 else freeT3_ve2)
                b.wait("tensor", se, wbev)
                bmw = b.mm(bk[:, :], sel_sb[:, e_ * 128:(e_ + 1) * 128],
                           wfm[:, th * 512:(th + 1) * 512], start=True, stop=True, ctr=pe)
                b.wait("scalar", pe, bmw)
                wbev = b.scopy(wb_bf[:, e_ * 512:(e_ + 1) * 512], bk[:, :], ctr=se)
            wb_done = wbev
            freeT2_se = wb_done

            b.wait("vector", se, u2_memset_gate_se)
            b.wait("vector", pe, u2w_gate_pe)
            for j in range(3):
                zz = b.vop(lambda e, o=U2[j][:, :]: e.memset(o, 0.0),
                           ctr=ve if j == 2 else None)
            u2z_done = ve.n

            prev_silu = [wb_done] * NE
            prev_act = [0] * NE
            first_unit = True
            for sw in range(2):
                fh = sw * 22
                lora_ld = b2s_ld

                for ft in range(22):
                    Fg = fh + ft
                    bi = ft % 2
                    b.wait("gpsimd", pe, w1_free[bi])
                    b.dma("gpsimd", gd, w1b[bi].rearrange("p (dt m) -> p dt m", dt=NDT),
                          w1Tb[Fg])
                    b.dma("gpsimd", gd, w3b[bi].rearrange("p (dt m) -> p dt m", dt=NDT),
                          w3Tb[Fg])
                    wld = gd.n

                    b.wait("tensor", gd, wld)
                    b.wait("tensor", sd, ln2th_ld)
                    b.wait("tensor", se, freeT0_se)
                    b.wait("tensor", se, u_done)
                    for dt in range(NDT):
                        b.mm(T0[:, :], w1b[bi][:, dt * 128:(dt + 1) * 128],
                             arena[:, dt * 512:(dt + 1) * 512],
                             start=(dt == 0), stop=(dt == NDT - 1),
                             ctr=pe if dt == NDT - 1 else None)
                    p_b1 = pe.n
                    b.wait("scalar", pe, p_b1)
                    b.wait("scalar", pe, base1_gate_pe)
                    b1_ev = b.scopy(base1_bf[:, :], T0[:, :], ctr=se)
                    b.wait("tensor", ve, freeT1_ve)
                    for dt in range(NDT):
                        b.mm(T1[:, :], w3b[bi][:, dt * 128:(dt + 1) * 128],
                             arena[:, dt * 512:(dt + 1) * 512],
                             start=(dt == 0), stop=(dt == NDT - 1),
                             ctr=pe if dt == NDT - 1 else None)
                    p_b3 = pe.n
                    b.wait("scalar", pe, p_b3)
                    b.wait("scalar", pe, base3_gate_pe)
                    b3_ev = b.scopy(base3_bf[:, :], T1[:, :], ctr=se)
                    w1_free[bi] = p_b3

                    b.wait("tensor", gd, lora_ld)
                    b.wait("tensor", gd, uz_ld)
                    b.wait("tensor", gd, b2s_ld)
                    silu_cnt = [0] * NE
                    act_cnt = [0] * NE
                    wact_cnt = [0] * NE
                    h3_cnt = [0] * NE
                    mixu_cnt = [0] * NE

                    def emit_mixu(w, _sw=sw, _ft=ft, _Fg=Fg):
                        b.wait("tensor", ve, wact_cnt[w])
                        if w == 0:
                            b.wait("tensor", se, mix_start_se)
                            b.wait("tensor", ve, u2z_done)
                        b.mm(MIXB[:, :], I_bf[:, :], wactb[w % 2][:, :],
                             start=(w == 0), stop=(w == NE - 1))
                        j, kk = w // 3, w % 3
                        mixu_cnt[w] = b.mm(
                            U2[j][32 * kk:32 * kk + 16, :],
                            a2h[:, (w * NFT + _Fg) * 16:(w * NFT + _Fg + 1) * 16],
                            wactb[w % 2][:, :],
                            start=False, stop=(_sw == 1 and _ft == 21),
                            ctr=pe, sgc=True)

                    for e_ in range(NE):
                        h1b = T2 if e_ % 2 == 0 else T0
                        h3b = T3 if e_ % 2 == 0 else T1
                        b.wait("tensor", se, b1_ev)
                        if e_ % 2 == 0:
                            b.wait("tensor", se,
                                   silu_cnt[e_ - 2] if e_ >= 2 else prev_silu[6])
                            if first_unit or e_ == 0:
                                b.wait("tensor", ve, freeT2_ve)
                        else:
                            b.wait("tensor", se,
                                   silu_cnt[e_ - 2] if e_ >= 2 else b1_ev)
                        b.mm(h1b[:, :], I_bf[:, :], base1_bf[:, :], start=True, stop=False)
                        h1c = b.mm(h1b[:, :], b1h[:, Fg * 128:(Fg + 1) * 128],
                                   arena[:, 8192 + e_ * 512:8192 + (e_ + 1) * 512],
                                   start=False, stop=True, ctr=pe)
                        b.wait("tensor", se, b3_ev)
                        if e_ % 2 == 0:
                            b.wait("tensor", ve,
                                   act_cnt[e_ - 2] if e_ >= 2 else prev_act[6])
                            if first_unit or e_ == 0:
                                b.wait("tensor", ve, freeT3_ve2)
                        else:
                            b.wait("tensor", ve,
                                   act_cnt[e_ - 2] if e_ >= 2 else freeT1_ve)
                        b.mm(h3b[:, :], I_bf[:, :], base3_bf[:, :], start=True, stop=False)
                        h3_cnt[e_] = b.mm(h3b[:, :], b3h[:, Fg * 128:(Fg + 1) * 128],
                                          arena[:, 12288 + e_ * 512:12288 + (e_ + 1) * 512],
                                          start=False, stop=True, ctr=pe)
                        b.wait("scalar", pe, h1c)
                        if e_ >= 2:
                            b.wait("scalar", ve, act_cnt[e_ - 2])
                        silu_cnt[e_] = b.act(s1b[e_ % 2][:, :], h1b[:, :], AF.Silu, ctr=se)
                        b.wait("vector", se, silu_cnt[e_])
                        b.wait("vector", se, wb_done)
                        b.tt(s1b[e_ % 2][:, :], s1b[e_ % 2][:, :],
                             wb_bf[:, e_ * 512:(e_ + 1) * 512], ALU.mult)
                        b.wait("vector", pe, h3_cnt[e_])
                        if e_ >= 2:
                            b.wait("vector", pe, mixu_cnt[e_ - 2])
                        act_cnt[e_] = b.tt(wactb[e_ % 2][:, :], s1b[e_ % 2][:, :],
                                           h3b[:, :], ALU.mult, ctr=ve)
                        wact_cnt[e_] = act_cnt[e_]
                        if e_ >= 1:
                            emit_mixu(e_ - 1)
                    emit_mixu(NE - 1)
                    first_unit = False
                    prev_silu = silu_cnt
                    prev_act = act_cnt
                    freeT0_se = silu_cnt[7]
                    freeT1_ve = act_cnt[7]
                    base1_gate_pe = h1c
                    base3_gate_pe = h3_cnt[7]
                    b.wait("scalar", pe, mixu_cnt[7])
                    b.wait("scalar", pe, mixbuf_gate_pe)
                    mix_start_se = b.scopy(mix[:, Fg * 512:(Fg + 1) * 512], MIXB[:, :],
                                           ctr=se)
                    mixevac_last = mix_start_se
            b.wait("scalar", pe, pe.n)
            for j in range(3):
                b.scopy(u2w[:, j * 512:(j + 1) * 512], U2[j][:, :], scale=2.0, ctr=se)
            u2w_done = se.n
            u2_memset_gate_se = u2w_done

            for dt in range(NDT):
                bi = dt % 2
                b.wait("gpsimd", pe, w2_free[bi])
                b.dma("gpsimd", gd, w2st[bi].rearrange("p (ft m) -> p ft m", ft=NFT),
                      w2Tb[dt])
                w2ld = gd.n
                b.wait("sync", od, prev_store[bi])
                b.dma("sync", sd, x2ch[bi][:, :],
                      x2s[dt * 128:(dt + 1) * 128, th * 512:(th + 1) * 512])
                x2ld = sd.n
                pb = T2 if dt % 2 == 0 else T3
                b.wait("tensor", gd, w2ld)
                b.wait("tensor", se, mixevac_last)
                b.wait("tensor", se, u2w_done)
                b.wait("tensor", ve, prev_out_ve[dt % 2])
                if dt == 0:
                    b.wait("tensor", se, prev_silu[6])
                    b.wait("tensor", ve, prev_act[6])
                for ftg in range(NFT):
                    b.mm(pb[:, :], w2st[bi][:, ftg * 128:(ftg + 1) * 128],
                         mix[:, ftg * 512:(ftg + 1) * 512],
                         start=(ftg == 0), stop=False)
                for j in range(3):
                    b.mm(pb[:, :], b2s[:, (j * NDT + dt) * 128:(j * NDT + dt + 1) * 128],
                         u2w[:, j * 512:(j + 1) * 512],
                         start=False, stop=(j == 2), ctr=pe if j == 2 else None)
                pw = pe.n
                b.wait("vector", pe, pw)
                b.wait("vector", sd, x2ld)
                ovo = b.tt(x2ch[bi][:, :], pb[:, :], x2ch[bi][:, :], ALU.add, ctr=ve)
                prev_out_ve[dt % 2] = ovo
                if dt % 2 == 0:
                    freeT2_ve = ovo
                else:
                    freeT3_ve2 = ovo
                b.wait("sync", ve, ovo)
                b.dma("sync", od, out_h[dt * 128:(dt + 1) * 128, th * 512:(th + 1) * 512],
                      x2ch[bi][:, :])
                prev_store[bi] = od.n
                w2_free[bi] = pw
            mixbuf_gate_pe = pe.n
            u2w_gate_pe = pe.n
            ln2th_free = pe.n
            uz_free = pe.n
            wb_gate_ve = ve.n
        st_K.close()
        st_H.close()

        b.wait("sync", od, od.n)
        assert od.n == 16 * 32, od.n

        nops = {k: len(v) for k, v in b.ops.items()}
        print("kernel build: ops per engine:", nops, flush=True)

        with nc.Block() as block:
            @block.sync
            def _(e):
                for f in b.ops["sync"]:
                    f(e)

            @block.gpsimd
            def _(e):
                for f in b.ops["gpsimd"]:
                    f(e)

            @block.tensor
            def _(e):
                for f in b.ops["tensor"]:
                    f(e)

            @block.scalar
            def _(e):
                for f in b.ops["scalar"]:
                    f(e)

            @block.vector
            def _(e):
                for f in b.ops["vector"]:
                    f(e)

    return nc


_PROG_CACHE = {}


def kernel(**inputs):
    x = np.ascontiguousarray(np.asarray(inputs["L0_attn_x"], dtype=np.float32))
    wq = np.asarray(inputs["L0_attn_wq"], dtype=np.float32)
    wk = np.asarray(inputs["L0_attn_wk"], dtype=np.float32)
    wv = np.asarray(inputs["L0_attn_wv"], dtype=np.float32)
    wo = np.asarray(inputs["L0_attn_wo"], dtype=np.float32)
    bo = np.asarray(inputs["L0_attn_bo"], dtype=np.float32)
    gate_w = np.asarray(inputs["L0_ffn_gate_w"], dtype=np.float32)
    w1 = np.asarray(inputs["L0_ffn_w1"], dtype=np.float32)
    w3 = np.asarray(inputs["L0_ffn_w3"], dtype=np.float32)
    w2 = np.asarray(inputs["L0_ffn_w2"], dtype=np.float32)
    a1 = np.asarray(inputs["L0_ffn_a1"], dtype=np.float32)
    b1 = np.asarray(inputs["L0_ffn_b1"], dtype=np.float32)
    a3 = np.asarray(inputs["L0_ffn_a3"], dtype=np.float32)
    b3 = np.asarray(inputs["L0_ffn_b3"], dtype=np.float32)
    a2 = np.asarray(inputs["L0_ffn_a2"], dtype=np.float32)
    b2 = np.asarray(inputs["L0_ffn_b2"], dtype=np.float32)

    asc = np.ascontiguousarray
    shared = dict(
        wqTb=asc(wq.T.reshape(NDT, 128, 4, 512).transpose(2, 1, 0, 3)),
        wkTb=asc(wk.T.reshape(NDT, 128, 4, 512).transpose(2, 1, 0, 3)),
        wvTb=asc(wv.T.reshape(NDT, 128, 4, 512).transpose(2, 1, 0, 3)),
        woTb=asc(wo.T.reshape(NDT, 128, NDT, 128).transpose(2, 1, 0, 3)),
        bo1p=asc(bo.reshape(1, D)),
        gateT=asc(gate_w.T.reshape(NDT, 128, NE).transpose(1, 0, 2)),
        w1Tb=asc(w1.T.reshape(NDT, 128, NFT, 128).transpose(2, 1, 0, 3)),
        w3Tb=asc(w3.T.reshape(NDT, 128, NFT, 128).transpose(2, 1, 0, 3)),
        w2Tb=asc(w2.T.reshape(NFT, 128, NDT, 128).transpose(2, 1, 0, 3)),
        a1sT=asc(a1.reshape(NE * 16, D).T.reshape(NDT, 128, 128).transpose(1, 0, 2)),
        a3sT=asc(a3.reshape(NE * 16, D).T.reshape(NDT, 128, 128).transpose(1, 0, 2)),
        b1sT=asc(b1.transpose(0, 2, 1).reshape(NE * 16, FF)),
        b3sT=asc(b3.transpose(0, 2, 1).reshape(NE * 16, FF)),
        a2b=asc(np.stack([a2[e_].T.reshape(NFT, 128, 16).transpose(1, 0, 2)
                          for e_ in range(NE)])),
        identin=np.eye(128, dtype=np.float32),
        onescolin=np.ones((128, 1), np.float32),
        onesrowin=np.ones((1, 512), np.float32),
    )
    b2stk = np.zeros((3, 128, D), np.float32)
    for e_ in range(NE):
        b2stk[e_ // 3, 32 * (e_ % 3):32 * (e_ % 3) + 16, :] = b2[e_].T
    shared["b2stk"] = asc(b2stk)
    ii = np.arange(128)
    tri = np.zeros((128, 128), np.float32)
    tri[ii[:, None] > ii[None, :]] = NEG
    full = np.full((128, 128), NEG, np.float32)
    zer = np.zeros((128, 128), np.float32)
    shared["trimaskin"] = np.stack([
        np.concatenate([full] * j + [tri] + [zer] * (3 - j), 1) for j in range(4)])
    sel = np.zeros((128, NE * 128), np.float32)
    for e_ in range(NE):
        sel[e_, e_ * 128:(e_ + 1) * 128] = 1.0
    shared["selin"] = sel

    full = np.full((128, 128), NEG, np.float32)
    zer = np.zeros((128, 128), np.float32)
    if "prog" not in _PROG_CACHE:
        _PROG_CACHE["prog"] = build_program()
    nc = _PROG_CACHE["prog"]

    in_maps = []
    orders = []
    for c in range(8):
        bb, p = c // 2, c % 2
        order = [2 * i + p for i in range(8)] + [2 * i + (1 - p) for i in range(8)]
        orders.append(order)
        xTb = asc(x[bb].T.reshape(D, 16, 128)[:, order, :].reshape(D, S))
        if p == 0:
            dm = np.stack([
                np.concatenate([full] * (jj + 1) + [zer] * (3 - jj), 1)
                for jj in range(4)])
        else:
            dm = np.stack([
                np.concatenate(([full] * jj + [zer] * (4 - jj)) if jj else [zer] * 4, 1)
                for jj in range(4)])
        m = {"xT": xTb, "dmask": dm}
        m.update(shared)
        in_maps.append(m)

    res = run_bass_kernel_spmd(nc, in_maps, core_ids=list(range(8)))
    outp = np.empty((4, S, D), np.float32)
    for c in range(8):
        bb, p = c // 2, c % 2
        o = np.asarray(res.results[c]["out"])
        ot = np.ascontiguousarray(o.T)
        for i in range(8):
            g = orders[c][i]
            outp[bb, g * 128:(g + 1) * 128, :] = ot[i * 128:(i + 1) * 128, :]
    return outp
